# version 06
# baseline (speedup 1.0000x reference)
import math

import numpy as np

import concourse.bass as bass
import concourse.mybir as mybir
from concourse.bass_utils import run_bass_kernel_spmd

F32 = mybir.dt.float32
BF16 = mybir.dt.bfloat16
AF = mybir.ActivationFunctionType
ALU = mybir.AluOpType
AX = mybir.AxisListType

D = 2048
KT = 16
NH = 16
DK = 128
E = 8
HR = 1024
HS = 8192
HST = HS // 128
C = 64
LN_EPS = 1e-5
RMS_EPS = 1e-5
CLAMP = 85.0


class MK:
    def __init__(self, S_CORE=4096, TH=512, TT=1024, taps=()):
        assert S_CORE % TT == 0 and TH % C == 0 and TT % 512 == 0
        assert TH <= TT <= 1024
        self.S = S_CORE
        self.TH = TH
        self.TT = TT
        self.NTT = S_CORE // TT
        self.NCH = TT // C
        self.taps = list(taps)
        self.nc = bass.Bass("TRN2", target_bir_lowering=False,
                            detect_race_conditions=False)
        self.cnt = {}
        self.sems = {}
        self.wfree = [("tp", 0), ("tp", 0)]
        self.xstrip_free = [("tp", 0), ("tp", 0)]
        self.vstrip_free = [("tp", 0), ("tp", 0)]
        self.ostrip_free = [("tp", 0), ("tp", 0)]

    def sem(self, name):
        if name not in self.sems:
            self.sems[name] = self.es.enter_context(self.nc.semaphore(name))
            self.cnt[name] = 0
        return self.sems[name]

    def inc(self, instr, name, k=1):
        instr.then_inc(self.sem(name), k)
        self.cnt[name] += k
        return self.cnt[name]

    def val(self, name):
        self.sem(name)
        return self.cnt[name]

    def w(self, eng, name, v=None):
        eng.wait_ge(self.sem(name), self.cnt[name] if v is None else v)

    def on(self, engname, fn):
        getattr(self.block, engname)(fn)

    def tap_bf(self, name, ap, dst_ap_fn=None):
        if name not in self.tap_d:
            return
        mk = self
        nv = self.val("vp")
        ns = self.val("sp")
        nt = self.val("tp")
        dst = self.tap_d[name].ap() if dst_ap_fn is None else \
            dst_ap_fn(self.tap_d[name])

        def sy_t(g):
            mk.w(g, "vp", nv)
            mk.w(g, "sp", ns)
            mk.w(g, "tp", nt)
            mk.inc(g.dma_start(dst, ap), "d_tap", 16)
        self.on("gpsimd", sy_t)

    def fence(self):
        tp, vp, sp = self.val("tp"), self.val("vp"), self.val("sp")

        def t_f(t):
            t.wait_ge(self.sem("vp"), vp)
            t.wait_ge(self.sem("sp"), sp)
        self.on("tensor", t_f)

        def v_f(v):
            v.wait_ge(self.sem("tp"), tp)
            v.wait_ge(self.sem("sp"), sp)
        self.on("vector", v_f)

        def s_f(s):
            s.wait_ge(self.sem("tp"), tp)
            s.wait_ge(self.sem("vp"), vp)
        self.on("scalar", s_f)

    def build(self):
        nc = self.nc
        S, TH, TT = self.S, self.TH, self.TT
        self.x_d = nc.dram_tensor("x", [D, TH + S], F32, kind="ExternalInput")
        self.wdr = {n: nc.dram_tensor(n, [D, D], F32, kind="ExternalInput")
                    for n in ("wq", "wf", "wi", "wo")}
        self.gwrow_d = nc.dram_tensor("gwrow", [1, D], F32,
                                      kind="ExternalInput")
        self.sw1t_d = nc.dram_tensor("sw1t", [D, HS], F32,
                                     kind="ExternalInput")
        self.sw2t_d = nc.dram_tensor("sw2t", [D, HS], F32,
                                     kind="ExternalInput")
        self.sw3t_d = nc.dram_tensor("sw3t", [HS, D], F32,
                                     kind="ExternalInput")
        self.w12t_d = nc.dram_tensor("w12t", [E, D, 2 * HR], F32,
                                     kind="ExternalInput")
        self.w3t_d = nc.dram_tensor("w3t", [E, HR, D], F32,
                                    kind="ExternalInput")
        self.rwt_d = nc.dram_tensor("rwt", [D, E], F32, kind="ExternalInput")
        self.ebias_d = nc.dram_tensor("ebias", [128, E], F32,
                                      kind="ExternalInput")
        self.halo_d = nc.dram_tensor("halo_on", [128, 1], F32,
                                     kind="ExternalInput")
        self.out_d = nc.dram_tensor("out", [D, S], F32, kind="ExternalOutput")
        self.x1_d = nc.dram_tensor("x1s", [D, S], F32)
        self.vb_d = nc.dram_tensor("vbounce", [TT, D], BF16)
        self.p_d = nc.dram_tensor("pscratch", [HS, S], BF16)
        self.ffn_d = nc.dram_tensor("ffnacc", [D, S], F32)
        self.tap_d = {}
        for name, shape in self.taps:
            self.tap_d[name] = nc.dram_tensor(name, list(shape), F32,
                                              kind="ExternalOutput")

        from contextlib import ExitStack
        self.es = ExitStack()
        with self.es:
            self._alloc()
            self.block = self.es.enter_context(nc.Block())
            self._emit()
        return nc

    def _alloc(self):
        nc, es = self.nc, self.es
        TT = self.TT
        sb = lambda n, s, d: es.enter_context(nc.sbuf_tensor(n, s, d))
        ps = lambda n, s, d: es.enter_context(nc.psum_tensor(n, s, d))
        self.ident_bf = sb("ident_bf", [128, 128], BF16)
        self.maskr = sb("maskr", [C, 8 * C], BF16)
        self.cbf = sb("cbf", [128, 32], BF16)
        self.ones_bf = sb("ones_bf", [128, 1], BF16)
        self.ones1_f = sb("ones1_f", [1, 128], F32)
        self.c_eps = sb("c_eps", [128, 1], F32)
        self.c_reps = sb("c_reps", [128, 1], F32)
        self.c_invd = sb("c_invd", [128, 1], F32)
        self.c_lnsc = sb("c_lnsc", [128, 1], F32)
        self.zeros_c = sb("zeros_c", [128, C], F32)
        self.gw_rep = sb("gw_rep", [128, D], BF16)
        self.ebias_sb = sb("ebias_sb", [128, E], F32)
        self.halo_sb = sb("halo_sb", [128, 1], F32)
        self.rwt_sb = sb("rwt_sb", [128, KT * E], BF16)
        self.h_st = sb("h_st", [128, NH * DK], F32)
        self.h_bf = sb("h_bf", [128, NH * DK], BF16)
        self.ech = sb("ech", [128, NH * self.NCH], F32)
        self.am_sb = sb("am_sb", [C, 8 * C], BF16)
        self.ktm_sb = sb("ktm_sb", [C, 8 * DK], BF16)
        self.otmp = sb("otmp", [C, D], BF16)
        self.rms_sc = sb("rms_sc", [C, 4], F32)
        self.dbg1 = sb("dbg1", [128, 64], BF16)
        self.dbg2 = sb("dbg2", [64, 128], BF16)
        self.tmp1 = sb("tmp1", [128, TT], F32)
        self.strow = sb("strow", [1, 1536], F32)
        self.tmp2 = None
        self.tmp3 = None
        self.xbfk = sb("xbfk", [128, 1024], BF16)
        self.sqk = sb("sqk", [128, 1024], BF16)
        self.xstrip = sb("xstrip", [128, 2048], F32)
        self.ostrip = sb("ostrip", [128, 1024], F32)
        self.tmp2 = self.xstrip[:, 0:1024]
        self.tmp3 = self.xstrip[:, 1024:2048]
        self.sc_sb = sb("sc_sb", [128, 512], F32)
        self.crow = sb("crow", [1, 2 * 1024], BF16)
        self.cT_sb = sb("cT_sb", [8, 1024], BF16)
        self.ones1_bf = sb("ones1_bf", [1, 128], BF16)
        self.P1 = sb("P1", [128, 8192], F32)
        self.P2 = sb("P2", [128, 8192], F32)
        self.P3 = sb("P3", [128, 8192], F32)
        self.P4 = sb("P4", [128, 8192], F32)
        self.WP = sb("WP", [128, 4096], F32)
        self.ps_a = ps("ps_a", [128, 1024], F32)
        self.ps_b = ps("ps_b", [128, 1024], F32)
        self.ps_c = ps("ps_c", [128, 1024], F32)
        self.ps_tr2 = ps("ps_tr2", [64, 1024], BF16)
        self.ps_tr = ps("ps_tr", [128, 1024], BF16)

        self.xln_v = self.P1.ap().bitcast(BF16).rearrange(
            "p (k t) -> p k t", k=KT)
        self.ofm_v = self.P1.ap().bitcast(BF16).rearrange(
            "p (k t) -> p k t", k=KT)
        self.qt_v = self.P2.ap().bitcast(BF16).rearrange(
            "p (h t) -> p h t", h=NH)
        self.ao_v = self.P2.ap().bitcast(BF16).rearrange(
            "p (k t) -> p k t", k=KT)
        self.kt_v = self.P3.ap().bitcast(BF16).rearrange(
            "p (h t) -> p h t", h=NH)
        self.v2_v = self.P4[0:C, :].bitcast(BF16).rearrange(
            "p (c d) -> p c d", d=D)
        self.y_v = self.xln_v
        self.cbc_v = self.P2.ap().rearrange("p (e t) -> p e t", e=E)
        self.hq_v = self.P3.ap().bitcast(BF16).rearrange(
            "p (i t) -> p i t", i=16)
        self.s1_v = self.P4.ap().bitcast(BF16).rearrange(
            "p (i t) -> p i t", i=16)
        self.wv = self.WP.ap().bitcast(BF16)
        self.sw3a_v = self.P1.ap().bitcast(BF16).rearrange(
            "p (h t) -> p h t", h=32)
        self.sw3b_v = self.P2.ap().bitcast(BF16).rearrange(
            "p (h t) -> p h t", h=32)
        self.pb4_v = [self.P3.ap().bitcast(BF16).rearrange(
            "p (h t) -> p h t", h=HST),
            self.P4.ap().bitcast(BF16).rearrange(
                "p (h t) -> p h t", h=HST)]

    def wvv(self):
        return self.wv.rearrange("p (s k w) -> p s k w", s=2, k=KT)

    def _emit(self):
        self._consts()
        self._attn_tile(is_halo=True, tile=0)
        for t in range(self.NTT):
            self.fence()
            self._attn_tile(is_halo=False, tile=t)
        for t in range(self.NTT):
            self.fence()
            self._ffn_tile(t)
        self.fence()
        self._b4()
        self.fence()
        self._b6()
        self._finish()

    def _consts(self):
        mk = self

        def g_fn(g):
            g.memset(mk.ident_bf[:, :], 0.0)
            g.memset(mk.maskr[:, :], 1.0)
            g.memset(mk.ones_bf[:, :], 1.0)
            g.memset(mk.ones1_f[:, :], 1.0)
            g.memset(mk.ones1_bf[:, :], 1.0)
            g.memset(mk.c_eps[:, :], LN_EPS)
            g.memset(mk.c_reps[:, :], RMS_EPS)
            g.memset(mk.c_invd[:, :], 1.0 / D)
            g.memset(mk.c_lnsc[:, :], -0.5 * math.log(DK))
            g.memset(mk.zeros_c[:, :], 0.0)
            g.memset(mk.h_st[:, :], 0.0)
            mk.inc(g.memset(mk.h_bf[:, :], 0.0), "gp")
            mk.w(g, "gp")
            g.affine_select(out=mk.ident_bf[:, :], in_=mk.ident_bf[:, :],
                            compare_op=ALU.not_equal, fill=1.0, base=0,
                            pattern=[[-1, 128]], channel_multiplier=1)
            mk.inc(g.affine_select(
                out=mk.maskr.ap().rearrange("p (b t) -> p b t", b=8),
                in_=mk.maskr.ap().rearrange("p (b t) -> p b t", b=8),
                compare_op=ALU.is_ge, fill=0.0, base=0,
                pattern=[[0, 8], [1, C]], channel_multiplier=-1), "gp")
            mk.inc(g.dma_start(
                mk.rwt_sb.ap().rearrange("p (k e) -> p k e", k=KT),
                mk.rwt_d.ap().rearrange("(k p) e -> p k e", p=128)),
                "d_cst", 16)
            mk.inc(g.dma_start(mk.ebias_sb[:, :], mk.ebias_d[:, :]),
                   "d_cst", 16)
            mk.inc(g.dma_start(mk.halo_sb[:, :], mk.halo_d[:, :]),
                   "d_cst", 16)
        self.on("gpsimd", g_fn)
        self.gp_consts = self.val("gp")
        self.d_cst_all = self.val("d_cst")

        def sy_fn(sy):
            mk.inc(sy.dma_start(mk.xstrip[0:1, 0:D], mk.gwrow_d[0:1, :]),
                   "d_gw", 16)
        self.on("sync", sy_fn)

        def t_fn(t):
            mk.w(t, "d_gw", 16)
            mk.w(t, "gp", mk.gp_consts)
            for n in range(4):
                dst = (mk.ps_a if n < 2 else mk.ps_b)
                mk.inc(t.matmul(dst[:, (n % 2) * 512:(n % 2) * 512 + 512],
                                mk.ones1_f[:, :],
                                mk.xstrip[0:1, n * 512:(n + 1) * 512],
                                start=True, stop=True), "tp")
        self.on("tensor", t_fn)
        tpv = self.val("tp")

        def v_fn(v):
            mk.w(v, "tp", tpv)
            v.tensor_copy(mk.gw_rep[:, 0:1024], mk.ps_a[:, :])
            mk.inc(v.tensor_copy(mk.gw_rep[:, 1024:2048], mk.ps_b[:, :]),
                   "vp")
        self.on("vector", v_fn)
        self.ps_ab_free = ("vp", self.val("vp"))
        self.xstrip_free = [self.ps_ab_free, self.ps_ab_free]

    def _ln_stream(self, src_dram, base, stride, tlen, dst_v):
        mk = self
        nsp = (tlen + 511) // 512
        spans = [(i * 512, min(512, tlen - i * 512)) for i in range(nsp)]
        st_free = [self.xstrip_free[0], self.xstrip_free[1]]
        qsem = ["d_x0", "d_xg"]
        qeng = ["sync", "gpsimd"]
        tmp1bf = self.tmp1.ap().bitcast(BF16)
        xb = [mk.xbfk, tmp1bf[:, 0:1024]]
        sq = [mk.sqk, tmp1bf[:, 1024:2048]]
        xs_free = [[st_free[0]], [st_free[1]]]
        stat_free = [st_free[0], st_free[1]]
        vp_tmp1 = self.val("vp")
        for k in range(KT):
            slot = k % 2

            def sy_fn(sy, k=k, slot=slot, xf=tuple(xs_free[slot])):
                for sem, vv in xf:
                    sy.wait_ge(mk.sem(sem), vv)
                src = bass.AP(src_dram, base + (128 * k) * stride,
                              [[stride, 128], [1, tlen]])
                mk.inc(sy.dma_start(
                    mk.xstrip[:, slot * 1024:slot * 1024 + tlen], src),
                    qsem[slot], 16)
            self.on(qeng[slot], sy_fn)
            ndx = self.val(qsem[slot])

            def v_fn(v, k=k, slot=slot, ndx=ndx, stf=tuple(stat_free[slot])):
                mk.w(v, qsem[slot], ndx)
                v.wait_ge(mk.sem(stf[0]), stf[1])
                if k < 2:
                    for sem2, vv2 in (self.vstrip_free[0],
                                      self.vstrip_free[1]):
                        v.wait_ge(mk.sem(sem2), vv2)
                mk.inc(v.tensor_copy(
                    xb[slot][:, 0:tlen],
                    mk.xstrip[:, slot * 1024:slot * 1024 + tlen]), "vp")
            self.on("vector", v_fn)
            nv = self.val("vp")

            def s_fn(s, k=k, slot=slot, ndx=ndx, stf=tuple(stat_free[slot])):
                mk.w(s, qsem[slot], ndx)
                s.wait_ge(mk.sem(stf[0]), stf[1])
                if k < 2:
                    mk.w(s, "vp", vp_tmp1)
                mk.inc(s.activation(
                    sq[slot][:, 0:tlen],
                    mk.xstrip[:, slot * 1024:slot * 1024 + tlen],
                    AF.Square), "sp")
            self.on("scalar", s_fn)
            ns = self.val("sp")

            def t_fn(t, k=k, slot=slot, nv=nv, ns=ns):
                mk.w(t, "vp", nv)
                mk.w(t, "sp", ns)
                if k == 0:
                    mk.w(t, *mk.ps_ab_free)
                last = None
                for sp0, sl in spans:
                    last = t.matmul(
                        mk.ps_a[0:1, sp0:sp0 + sl], mk.ones_bf[:, :],
                        xb[slot][:, sp0:sp0 + sl],
                        start=(k == 0), stop=(k == KT - 1))
                    last = t.matmul(
                        mk.ps_b[0:1, sp0:sp0 + sl], mk.ones_bf[:, :],
                        sq[slot][:, sp0:sp0 + sl],
                        start=(k == 0), stop=(k == KT - 1))
                mk.inc(last, "tp")
            self.on("tensor", t_fn)
            stat_free[slot] = ("tp", self.val("tp"))
            xs_free[slot] = [("vp", nv), ("sp", ns)]
        tp_stats = self.val("tp")

        tp_bc = None
        for sp0, sl in spans:
            def v_fn2(v, sp0=sp0, sl=sl, tp_prev=tp_bc):
                mk.w(v, "tp", tp_stats)
                if tp_prev is not None:
                    v.wait_ge(mk.sem("tp"), tp_prev)
                v.tensor_scalar(mk.strow[0:1, 0:sl], mk.ps_a[0:1, sp0:sp0 + sl],
                                1.0 / D, None, ALU.mult)
                v.tensor_scalar(mk.strow[0:1, 512:512 + sl],
                                mk.ps_b[0:1, sp0:sp0 + sl], 1.0 / D, None,
                                ALU.mult)
                v.tensor_tensor(mk.strow[0:1, 1024:1024 + sl],
                                mk.strow[0:1, 0:sl], mk.strow[0:1, 0:sl],
                                ALU.mult)
                mk.inc(v.tensor_tensor(mk.strow[0:1, 512:512 + sl],
                                       mk.strow[0:1, 512:512 + sl],
                                       mk.strow[0:1, 1024:1024 + sl],
                                       ALU.subtract), "vp")
            self.on("vector", v_fn2)
            nvv = self.val("vp")

            def s_fn2(s, sl=sl, nvv=nvv):
                mk.w(s, "vp", nvv)
                mk.inc(s.activation(mk.strow[0:1, 1024:1024 + sl],
                                    mk.strow[0:1, 512:512 + sl], AF.Sqrt,
                                    bias=mk.c_eps[0:1, 0:1]), "sp")
            self.on("scalar", s_fn2)
            nss = self.val("sp")

            def v_fn3(v, sl=sl, nss=nss):
                mk.w(v, "sp", nss)
                mk.inc(v.reciprocal(mk.strow[0:1, 512:512 + sl],
                                    mk.strow[0:1, 1024:1024 + sl]), "vp")
            self.on("vector", v_fn3)
            nvr = self.val("vp")

            def t_fn2(t, sp0=sp0, sl=sl, nvr=nvr):
                mk.w(t, "vp", nvr)
                t.matmul(mk.ps_a[:, sp0:sp0 + sl], mk.ones1_f[:, :],
                         mk.strow[0:1, 0:sl], start=True, stop=True)
                mk.inc(t.matmul(mk.ps_b[:, sp0:sp0 + sl], mk.ones1_f[:, :],
                                mk.strow[0:1, 512:512 + sl],
                                start=True, stop=True), "tp")
            self.on("tensor", t_fn2)
            tp_bc = self.val("tp")

        x2_free = [list(xs_free[0]), list(xs_free[1])]
        vlast = None
        for k in range(KT):
            slot = k % 2

            def sy_fn2(sy, k=k, slot=slot, xf=tuple(x2_free[slot])):
                for sem, vv in xf:
                    sy.wait_ge(mk.sem(sem), vv)
                src = bass.AP(src_dram, base + (128 * k) * stride,
                              [[stride, 128], [1, tlen]])
                mk.inc(sy.dma_start(
                    mk.xstrip[:, slot * 1024:slot * 1024 + tlen], src),
                    qsem[slot], 16)
            self.on(qeng[slot], sy_fn2)
            ndx = self.val(qsem[slot])

            def v_fn4(v, k=k, slot=slot, ndx=ndx):
                mk.w(v, qsem[slot], ndx)
                if k == 0:
                    mk.w(v, "tp", tp_bc)
                for sp0, sl in spans:
                    v.tensor_tensor(
                        mk.tmp1[:, 0:sl],
                        mk.xstrip[:, slot * 1024 + sp0:slot * 1024 + sp0 + sl],
                        mk.ps_a[:, sp0:sp0 + sl], ALU.subtract)
                    mk.inc(v.tensor_tensor(
                        dst_v[:, k, sp0:sp0 + sl], mk.tmp1[:, 0:sl],
                        mk.ps_b[:, sp0:sp0 + sl], ALU.mult), "vp")
            self.on("vector", v_fn4)
            x2_free[slot] = [("vp", self.val("vp"))]
            vlast = self.val("vp")
        self.xstrip_free = [x2_free[0][0], x2_free[1][0]]
        self.ps_ab_free = ("vp", vlast)
        return vlast

    def _proj(self, wd, wtotcols, rhs_v, rhs_ready, tlen, evac):
        mk = self
        nsp = tlen // 512
        psfree = {}
        for ei in range(wtotcols // 256):
            slot = ei % 2

            def g_fn(g, ei=ei, slot=slot):
                sem, vv = mk.wfree[slot]
                g.wait_ge(mk.sem(sem), vv)
                src = bass.AP(wd, ei * 256,
                              [[wtotcols, 128], [128 * wtotcols, KT],
                               [1, 256]])
                mk.inc(g.dma_start(mk.wvv()[:, slot, :, :], src),
                       "d_w%d" % slot, 16)
            self.on("gpsimd", g_fn)
            nw = self.val("d_w%d" % slot)

            for ot in range(2):
                oabs = ei * 2 + ot
                for sp in range(nsp):
                    pb = ((oabs * nsp + sp) % 2) * 512

                    def t_fn(t, ei=ei, slot=slot, ot=ot, sp=sp, pb=pb, nw=nw,
                             last_slot=(ot == 1 and sp == nsp - 1)):
                        mk.w(t, "d_w%d" % slot, nw)
                        mk.w(t, "vp", rhs_ready)
                        if pb in psfree:
                            t.wait_ge(mk.sem(psfree[pb][0]), psfree[pb][1])
                        last = None
                        for k in range(KT):
                            last = t.matmul(
                                mk.ps_c[:, pb:pb + 512],
                                mk.wvv()[:, slot, k, ot * 128:(ot + 1) * 128],
                                rhs_v[:, k, sp * 512:sp * 512 + 512],
                                start=(k == 0), stop=(k == KT - 1))
                        mk.inc(last, "tp")
                        if last_slot:
                            mk.wfree[slot] = ("tp", mk.cnt["tp"])
                    self.on("tensor", t_fn)
                    psfree[pb] = evac(oabs, sp, pb, self.val("tp"))

    def _attn_tile(self, is_halo, tile):
        mk = self
        TT, TH, S = self.TT, self.TH, self.S
        tlen = TH if is_halo else TT
        ncht = tlen // C
        ntb = tlen // 128
        nsp = tlen // 512
        base = 0 if is_halo else TH + tile * TT

        vp_ln = self._ln_stream(self.x_d, base, TH + S, tlen, self.xln_v)

        if not is_halo and tile == 0:
            self.tap_bf("t_xln", self.xln_v[:, :, 0:tlen],
                        lambda d: d.ap().rearrange("p (k t) -> p k t", k=KT))

        if not is_halo:
            vp_prev = self.val("vp")

            def evac_q(oabs, sp, pb, tpv):
                def s_fn(s, oabs=oabs, sp=sp, pb=pb, tpv=tpv):
                    mk.w(s, "tp", tpv)
                    mk.w(s, "vp", vp_prev)
                    mk.inc(s.activation(
                        mk.qt_v[:, oabs, sp * 512:sp * 512 + 512],
                        mk.ps_c[:, pb:pb + 512], AF.Silu), "sp")
                mk.on("scalar", s_fn)
                return ("sp", mk.val("sp"))
            self._proj(self.wdr["wq"], D, self.xln_v, vp_ln, tlen, evac_q)

        vs_free = self.vstrip_free
        psfree_v = {0: ("sp", self.val("sp")), 512: ("sp", self.val("sp"))}
        for ei in range(8):
            slot = ei % 2

            def g_fn(g, ei=ei, slot=slot):
                sem, vv = mk.wfree[slot]
                g.wait_ge(mk.sem(sem), vv)
                src = bass.AP(self.wdr["wi"], ei * 256,
                              [[D, 128], [128 * D, KT], [1, 256]])
                mk.inc(g.dma_start(mk.wvv()[:, slot, :, :], src),
                       "d_w%d" % slot, 16)
            self.on("gpsimd", g_fn)
            nw = self.val("d_w%d" % slot)

            for tb in range(ntb):
                pb = (tb % 2) * 512

                def t_fn(t, ei=ei, slot=slot, tb=tb, pb=pb, nw=nw,
                         last_slot=(tb == ntb - 1)):
                    mk.w(t, "d_w%d" % slot, nw)
                    mk.w(t, "vp", vp_ln)
                    if pb in psfree_v:
                        t.wait_ge(mk.sem(psfree_v[pb][0]), psfree_v[pb][1])
                    last = None
                    for k in range(KT):
                        last = t.matmul(
                            mk.ps_c[:, pb:pb + 256],
                            mk.xln_v[:, k, tb * 128:(tb + 1) * 128],
                            mk.wvv()[:, slot, k, :],
                            start=(k == 0), stop=(k == KT - 1))
                    mk.inc(last, "tp")
                    if last_slot:
                        mk.wfree[slot] = ("tp", mk.cnt["tp"])
                self.on("tensor", t_fn)
                tpv = self.val("tp")
                ss = (ei * ntb + tb) % 2

                def v_fn(v, pb=pb, tpv=tpv, ss=ss):
                    mk.w(v, "tp", tpv)
                    v.wait_ge(mk.sem(vs_free[ss][0]), vs_free[ss][1])
                    mk.inc(v.tensor_copy(
                        mk.xbfk[:, ss * 512:ss * 512 + 256],
                        mk.ps_c[:, pb:pb + 256]), "vp")
                self.on("vector", v_fn)
                psfree_v[pb] = ("vp", self.val("vp"))
                nvv = self.val("vp")

                def sy_fn(sy, ei=ei, tb=tb, ss=ss, nvv=nvv):
                    mk.w(sy, "vp", nvv)
                    dst = bass.AP(self.vb_d, tb * 128 * D + ei * 256,
                                  [[D, 128], [1, 256]])
                    mk.inc(sy.dma_start(
                        dst, mk.xbfk[:, ss * 512:ss * 512 + 256]),
                        "d_vb", 16)
                self.on("sync", sy_fn)
                vs_free[ss] = ("d_vb", self.val("d_vb"))
        dvb = self.val("d_vb")

        f_psfree = ("vp", self.val("vp"))
        ew_free = None
        for h in range(NH):
            ei = h // 2
            ot = h % 2
            slot = ei % 2
            if ot == 0:
                def g_fn2(g, ei=ei, slot=slot):
                    sem, vv = mk.wfree[slot]
                    g.wait_ge(mk.sem(sem), vv)
                    src = bass.AP(self.wdr["wf"], ei * 256,
                                  [[D, 128], [128 * D, KT], [1, 256]])
                    mk.inc(g.dma_start(mk.wvv()[:, slot, :, :], src),
                           "d_w%d" % slot, 16)
                self.on("gpsimd", g_fn2)
            nw = self.val("d_w%d" % slot)

            def t_fn2(t, h=h, ot=ot, slot=slot, nw=nw):
                mk.w(t, "d_w%d" % slot, nw)
                mk.w(t, "vp", vp_ln)
                if f_psfree is not None:
                    t.wait_ge(mk.sem(f_psfree[0]), f_psfree[1])
                for sp0 in range(0, tlen, 512):
                    sl = min(512, tlen - sp0)
                    last = None
                    for k in range(KT):
                        last = t.matmul(
                            mk.ps_c[:, sp0:sp0 + sl],
                            mk.wvv()[:, slot, k, ot * 128:(ot + 1) * 128],
                            mk.xln_v[:, k, sp0:sp0 + sl],
                            start=(k == 0), stop=(k == KT - 1))
                    mk.inc(last, "tp")
                if ot == 1:
                    mk.wfree[slot] = ("tp", mk.cnt["tp"])
            self.on("tensor", t_fn2)
            tpf = self.val("tp")
            vp_prev2 = self.val("vp")

            def s_fn3(s, tpf=tpf, ew=ew_free):
                mk.w(s, "tp", tpf)
                if ew is not None:
                    s.wait_ge(mk.sem(ew[0]), ew[1])
                s.activation(mk.tmp1[:, 0:tlen], mk.ps_c[:, 0:tlen],
                             AF.Sigmoid, scale=-1.0)
                s.activation(mk.tmp2[:, 0:tlen], mk.ps_c[:, 0:tlen],
                             AF.Sigmoid)
                mk.inc(s.activation(mk.tmp2[:, 0:tlen], mk.tmp2[:, 0:tlen],
                                    AF.Ln), "sp")
            self.on("scalar", s_fn3)
            f_psfree = ("sp", self.val("sp"))
            spg = self.val("sp")

            def v_fn5(v, h=h, spg=spg, vp_prev2=vp_prev2):
                mk.w(v, "sp", spg)
                mk.w(v, "gp", mk.gp_consts)
                if h == 0:
                    mk.w(v, "vp", vp_prev2)
                for cc in range(ncht):
                    v.reduce_sum(
                        mk.ech[:, h * mk.NCH + cc:h * mk.NCH + cc + 1],
                        mk.tmp2[:, cc * C:(cc + 1) * C], axis=AX.X)
                for cc in range(ncht):
                    v.tensor_tensor_scan(
                        mk.tmp3[:, cc * C:(cc + 1) * C],
                        mk.tmp2[:, cc * C:(cc + 1) * C],
                        mk.zeros_c[:, :], 0.0, ALU.add, ALU.add)
                mk.inc(v.tensor_scalar(mk.tmp2[:, 0:tlen], mk.tmp3[:, 0:tlen],
                                       -1.0, CLAMP, ALU.mult, ALU.min), "vp")
            self.on("vector", v_fn5)
            nvc = self.val("vp")

            def s_fn4(s, h=h, nvc=nvc, is_halo=is_halo):
                mk.w(s, "vp", nvc)
                s.activation(mk.tmp2[:, 0:tlen], mk.tmp2[:, 0:tlen], AF.Exp)
                if not is_halo:
                    s.activation(mk.tmp3[:, 0:tlen], mk.tmp3[:, 0:tlen],
                                 AF.Exp, bias=mk.c_lnsc[:, 0:1])
                mk.inc(s.activation(
                    mk.ech[:, h * mk.NCH:h * mk.NCH + ncht],
                    mk.ech[:, h * mk.NCH:h * mk.NCH + ncht], AF.Exp), "sp")
            self.on("scalar", s_fn4)
            nse = self.val("sp")

            def v_fn6(v, h=h, nse=nse, is_halo=is_halo):
                mk.w(v, "sp", nse)
                mk.inc(v.tensor_tensor(mk.kt_v[:, h, 0:tlen],
                                       mk.tmp1[:, 0:tlen],
                                       mk.tmp2[:, 0:tlen], ALU.mult), "vp")
                if not is_halo:
                    mk.inc(v.tensor_tensor(mk.qt_v[:, h, 0:tlen],
                                           mk.qt_v[:, h, 0:tlen],
                                           mk.tmp3[:, 0:tlen], ALU.mult),
                           "vp")
            self.on("vector", v_fn6)
            ew_free = ("vp", self.val("vp"))
        vp_ew = self.val("vp")
        self.xstrip_free = [("vp", vp_ew), ("vp", vp_ew)]
        if not is_halo and tile == 0:
            self.tap_bf("t_qt", self.qt_v[:, :, 0:tlen],
                        lambda d: d.ap().rearrange("p (k t) -> p k t", k=NH))
            self.tap_bf("t_kt", self.kt_v[:, :, 0:tlen],
                        lambda d: d.ap().rearrange("p (k t) -> p k t", k=NH))
            self.tap_bf("t_ech", self.ech[:, :])

        self._scan_tap = (not is_halo) and tile == 0
        self._scan(tlen, ncht, is_halo, vp_ew, dvb)
        self._scan_tap = False

        if not is_halo and tile == 0:
            self.tap_bf("t_ofm", self.ofm_v[:, :, 0:tlen],
                        lambda d: d.ap().rearrange("p (k t) -> p k t", k=KT))
            self.tap_bf("t_hst", self.h_st[:, :])
        if is_halo:
            def v_halo(v, tpn=self.val("tp"), vpn=self.val("vp"),
                       spn=self.val("sp")):
                mk.w(v, "tp", tpn)
                mk.w(v, "vp", vpn)
                mk.w(v, "sp", spn)
                v.tensor_scalar(mk.h_st[:, :], mk.h_st[:, :],
                                mk.halo_sb[:, 0:1], None, ALU.mult)
                mk.inc(v.tensor_copy(mk.h_bf[:, :], mk.h_st[:, :]), "vp")
            self.on("vector", v_halo)
            self.tap_bf("t_hhalo", self.h_st[:, :])
        else:
            self._post(tile)

    def _scan(self, tlen, ncht, is_halo, vp_ew, dvb):
        mk = self
        v2half_loaded = -1
        psA_free = ("vp", vp_ew)
        tpO_prev = self.val("tp")
        pend_tr = None
        sp0 = self.val("sp")
        h_ready = {0: ("sp", sp0), 1: ("sp", sp0)}
        psB_free = ("sp", sp0)
        for c in range(ncht):
            half = c // 8
            cl = c % 8
            if half != v2half_loaded:
                tp_now = self.val("tp")
                nld = min(8, ncht - half * 8)

                def sy_v2(sy, half=half, tp_now=tp_now, nld=nld):
                    mk.w(sy, "d_vb", dvb)
                    sy.wait_ge(mk.sem("tp"), tp_now)
                    src = bass.AP(self.vb_d, half * 8 * C * D,
                                  [[D, C], [C * D, nld], [1, D]])
                    mk.inc(sy.dma_start(mk.v2_v[:, 0:nld, :], src),
                           "d_v2", 16)
                self.on("sync", sy_v2)
                v2half_loaded = half
            nv2 = self.val("d_v2")

            for hh in range(2):
                H = list(range(hh * 8, hh * 8 + 8))

                if not is_halo:
                    def t_A(t, c=c, H=H, paf=psA_free, nv2=nv2):
                        t.wait_ge(mk.sem(paf[0]), paf[1])
                        mk.w(t, "d_v2", nv2)
                        last = None
                        for j, h in enumerate(H):
                            last = t.matmul(
                                mk.ps_a[0:C, j * 64:(j + 1) * 64],
                                mk.kt_v[:, h, c * C:(c + 1) * C],
                                mk.qt_v[:, h, c * C:(c + 1) * C],
                                start=True, stop=True)
                        mk.inc(last, "tp")
                    self.on("tensor", t_A)
                tpA = self.val("tp")

                if not is_halo:
                    def v_mk(v, c=c, tpA=tpA):
                        mk.w(v, "tp", tpA)
                        mk.inc(v.tensor_tensor(mk.am_sb[:, :],
                                               mk.ps_a[0:C, 0:512],
                                               mk.maskr[:, 0:512], ALU.mult),
                               "vp")
                    self.on("vector", v_mk)
                nvk = self.val("vp")
                psA_free = ("vp", nvk)

                sp_gate = max(h_ready[hh][1], psB_free[1])

                def t_O(t, c=c, cl=cl, H=H, nvk=nvk, nv2=nv2,
                        sp_gate=sp_gate, vp_ew=vp_ew, is_halo=is_halo):
                    mk.w(t, "sp", sp_gate)
                    if not is_halo:
                        mk.w(t, "vp", nvk)
                    else:
                        mk.w(t, "vp", vp_ew)
                    mk.w(t, "d_v2", nv2)
                    last = None
                    for j, h in enumerate(H):
                        if not is_halo:
                            t.matmul(mk.ps_b[0:C, j * 128:(j + 1) * 128],
                                     mk.qt_v[:, h, c * C:(c + 1) * C],
                                     mk.h_bf[:, h * DK:(h + 1) * DK],
                                     start=True, stop=False)
                            t.matmul(mk.ps_b[0:C, j * 128:(j + 1) * 128],
                                     mk.am_sb[0:C, j * C:(j + 1) * C],
                                     mk.v2_v[:, cl, h * DK:(h + 1) * DK],
                                     start=False, stop=True)
                        last = t.transpose(
                            mk.ps_tr2[0:C, j * 128:(j + 1) * 128],
                            mk.kt_v[:, h, c * C:(c + 1) * C],
                            mk.ident_bf[:, :])
                    mk.inc(last, "tp")
                self.on("tensor", t_O)
                tpO = self.val("tp")
                tpO_prev = tpO

                ntr_guard = None
                if hh == 0 and pend_tr is not None:
                    ntr_guard = pend_tr()
                    pend_tr = None

                def v_ev(v, tpO=tpO):
                    mk.w(v, "tp", tpO)
                    mk.inc(v.tensor_copy(mk.ktm_sb[:, :],
                                         mk.ps_tr2[0:C, :]), "vp")
                self.on("vector", v_ev)
                nve = self.val("vp")

                if not is_halo:
                    def s_ev(s, hh=hh, tpO=tpO, ntr_guard=ntr_guard):
                        mk.w(s, "tp", tpO)
                        if ntr_guard is not None:
                            mk.w(s, "tp", ntr_guard)
                        mk.inc(s.copy(
                            mk.otmp[0:C, hh * 1024:hh * 1024 + 1024],
                            mk.ps_b[0:C, 0:1024]), "sp")
                    self.on("scalar", s_ev)
                    psB_free = ("sp", self.val("sp"))
                if (not is_halo) and c == 0 and hh == 0 and \
                        getattr(self, "_scan_tap", False):
                    self.tap_bf("t_otmp0", mk.otmp[0:C, 0:1024])
                if (not is_halo) and c == 0 and hh == 1 and \
                        getattr(self, "_scan_tap", False):
                    self.tap_bf("t_otmp1", mk.otmp[0:C, :])
                    self.tap_bf("t_am0", mk.am_sb[:, :])
                    self.tap_bf("t_ktm0", mk.ktm_sb[:, :])

                def t_S(t, c=c, cl=cl, H=H, nve=nve):
                    mk.w(t, "vp", nve)
                    last = None
                    for j, h in enumerate(H):
                        last = t.matmul(
                            mk.ps_c[:, j * 128:(j + 1) * 128],
                            mk.ktm_sb[:, j * DK:(j + 1) * DK],
                            mk.v2_v[:, cl, h * DK:(h + 1) * DK],
                            start=True, stop=True)
                    mk.inc(last, "tp")
                self.on("tensor", t_S)
                tpS = self.val("tp")

                def v_h(v, c=c, hh=hh, H=H, tpS=tpS):
                    mk.w(v, "tp", tpS)
                    v.tensor_tensor(
                        mk.h_st[:, hh * 1024:hh * 1024 + 1024],
                        mk.h_st[:, hh * 1024:hh * 1024 + 1024],
                        mk.ps_c[:, 0:1024], ALU.add)
                    last = None
                    for j, h in enumerate(H):
                        last = v.tensor_scalar(
                            mk.h_st[:, h * DK:(h + 1) * DK],
                            mk.h_st[:, h * DK:(h + 1) * DK],
                            mk.ech[:, h * mk.NCH + c:h * mk.NCH + c + 1],
                            None, ALU.mult)
                    mk.inc(last, "vp")
                self.on("vector", v_h)
                nvh2 = self.val("vp")

                def s_hb(s, hh=hh, nvh2=nvh2):
                    mk.w(s, "vp", nvh2)
                    mk.inc(s.copy(
                        mk.h_bf[:, hh * 1024:hh * 1024 + 1024],
                        mk.h_st[:, hh * 1024:hh * 1024 + 1024]), "sp")
                self.on("scalar", s_hb)
                h_ready[hh] = ("sp", self.val("sp"))

            if not is_halo:
                def s_sq1(s):
                    s.activation(mk.sqk[0:C, :], mk.otmp[:, 0:1024],
                                 AF.Square)
                    mk.inc(s.activation(mk.xbfk[0:C, :],
                                        mk.otmp[:, 1024:2048], AF.Square),
                           "sp")
                self.on("scalar", s_sq1)
                n1 = self.val("sp")

                def v_red2(v, n1=n1):
                    mk.w(v, "sp", n1)
                    v.tensor_tensor(mk.sqk[0:C, :].bitcast(F32)
                                    if False else mk.sqk[0:C, :],
                                    mk.sqk[0:C, :], mk.xbfk[0:C, :], ALU.add)
                    mk.inc(v.reduce_sum(mk.rms_sc[:, 0:1], mk.sqk[0:C, :],
                                        axis=AX.X), "vp")
                self.on("vector", v_red2)
                n4 = self.val("vp")

                def s_rms(s, n4=n4):
                    mk.w(s, "vp", n4)
                    mk.inc(s.activation(mk.rms_sc[:, 1:2], mk.rms_sc[:, 0:1],
                                        AF.Sqrt, scale=mk.c_invd[0:C, 0:1],
                                        bias=mk.c_reps[0:C, 0:1]),
                           "sp")
                self.on("scalar", s_rms)
                n5 = self.val("sp")

                def v_recip(v, n5=n5):
                    mk.w(v, "sp", n5)
                    mk.inc(v.reciprocal(mk.rms_sc[:, 0:1], mk.rms_sc[:, 1:2]),
                           "vp")
                self.on("vector", v_recip)
                n6 = self.val("vp")

                def s_scale(s, n6=n6):
                    mk.w(s, "vp", n6)
                    mk.inc(s.activation(mk.otmp[:, :], mk.otmp[:, :], AF.Copy,
                                        scale=mk.rms_sc[:, 0:1]), "sp")
                self.on("scalar", s_scale)
                n7 = self.val("sp")

                def v_gw(v, n7=n7):
                    mk.w(v, "sp", n7)
                    mk.inc(v.tensor_tensor(mk.otmp[:, :], mk.otmp[:, :],
                                           mk.gw_rep[0:C, :], ALU.mult), "vp")
                self.on("vector", v_gw)
                nrm = self.val("vp")

                def emit_tr(c=c, nrm=nrm):
                    def t_tr(t, c=c, nrm=nrm):
                        mk.w(t, "vp", nrm)
                        last = None
                        for vt in range(KT):
                            last = t.transpose(
                                mk.ps_tr[:, vt * 64:(vt + 1) * 64],
                                mk.otmp[0:C, vt * 128:(vt + 1) * 128],
                                mk.ident_bf[0:C, 0:C])
                        mk.inc(last, "tp")
                    self.on("tensor", t_tr)
                    ntr = self.val("tp")

                    def v_tr(v, c=c, ntr=ntr):
                        mk.w(v, "tp", ntr)
                        for vt in range(KT):
                            v.tensor_copy(mk.ofm_v[:, vt, c * C:(c + 1) * C],
                                          mk.ps_tr[:, vt * 64:(vt + 1) * 64])
                        if c == 0 and getattr(mk, "_scan_tap", False):
                            v.tensor_copy(mk.dbg1[:, :], mk.ps_tr[:, 0:64])
                            v.tensor_copy(mk.dbg2[:, :], mk.otmp[0:C, 0:128])
                        mk.inc(v.engine_nop(), "vp")
                    self.on("vector", v_tr)
                    if c == 0 and getattr(self, "_scan_tap", False):
                        self.tap_bf("t_dbg1", mk.dbg1[:, :])
                        self.tap_bf("t_dbg2", mk.dbg2[:, :])
                        self.tap_bf("t_otmpr", mk.otmp[0:C, :])
                        self.tap_bf("t_ofm0", mk.ofm_v[:, 0:2, 0:C],
                                    lambda dd: dd.ap().rearrange(
                                        "p (k t) -> p k t", k=2))
                    return ntr
                if c == ncht - 1:
                    emit_tr()
                else:
                    pend_tr = emit_tr

    def _post(self, tile):
        mk = self
        TT, TH, S = self.TT, self.TH, self.S
        nsp = TT // 512
        base = TH + tile * TT
        vp_ofm = self.val("vp")
        self._ln2_first = True

        def evac_wo(oabs, sp, pb, tpv):
            sslot = (oabs * nsp + sp) % 2

            def s_fn(s, oabs=oabs, sp=sp, pb=pb, tpv=tpv, sslot=sslot):
                mk.w(s, "tp", tpv)
                s.activation(mk.ao_v[:, oabs, sp * 512:sp * 512 + 512],
                             mk.ps_c[:, pb:pb + 512], AF.Copy)
                mk.inc(s.activation(
                    mk.sqk[:, sslot * 512:sslot * 512 + 512],
                    mk.ps_c[:, pb:pb + 512], AF.Square), "sp")
            mk.on("scalar", s_fn)
            nsq = mk.val("sp")

            def t_fn(t, oabs=oabs, sp=sp, nsq=nsq, sslot=sslot):
                mk.w(t, "sp", nsq)
                if mk._ln2_first:
                    mk.w(t, *mk.ps_ab_free)
                    mk._ln2_first = False
                first = (oabs == 0)
                lastk = (oabs == KT - 1)
                t.matmul(mk.ps_a[0:1, sp * 512:sp * 512 + 512],
                         mk.ones_bf[:, :],
                         mk.ao_v[:, oabs, sp * 512:sp * 512 + 512],
                         start=first, stop=lastk)
                mk.inc(t.matmul(
                    mk.ps_b[0:1, sp * 512:sp * 512 + 512], mk.ones_bf[:, :],
                    mk.sqk[:, sslot * 512:sslot * 512 + 512],
                    start=first, stop=lastk), "tp")
            mk.on("tensor", t_fn)
            return ("tp", mk.val("tp"))

        self._proj(self.wdr["wo"], D, self.ofm_v, vp_ofm, TT, evac_wo)
        tp_st2 = self.val("tp")

        tp_bc2 = None
        for spq in range(nsp):
            sp0 = spq * 512

            def v_st2(v, sp0=sp0, tp_prev=tp_bc2):
                mk.w(v, "tp", tp_st2)
                if tp_prev is not None:
                    v.wait_ge(mk.sem("tp"), tp_prev)
                v.tensor_scalar(mk.strow[0:1, 0:512],
                                mk.ps_a[0:1, sp0:sp0 + 512],
                                1.0 / D, None, ALU.mult)
                v.tensor_scalar(mk.strow[0:1, 512:1024],
                                mk.ps_b[0:1, sp0:sp0 + 512], 1.0 / D, None,
                                ALU.mult)
                v.tensor_tensor(mk.strow[0:1, 1024:1536],
                                mk.strow[0:1, 0:512], mk.strow[0:1, 0:512],
                                ALU.mult)
                mk.inc(v.tensor_tensor(mk.strow[0:1, 512:1024],
                                       mk.strow[0:1, 512:1024],
                                       mk.strow[0:1, 1024:1536],
                                       ALU.subtract), "vp")
            self.on("vector", v_st2)
            nv = self.val("vp")

            def s_st2(s, nv=nv):
                mk.w(s, "vp", nv)
                mk.inc(s.activation(mk.strow[0:1, 1024:1536],
                                    mk.strow[0:1, 512:1024], AF.Sqrt,
                                    bias=mk.c_eps[0:1, 0:1]), "sp")
            self.on("scalar", s_st2)
            ns = self.val("sp")

            def v_st3(v, ns=ns):
                mk.w(v, "sp", ns)
                mk.inc(v.reciprocal(mk.strow[0:1, 512:1024],
                                    mk.strow[0:1, 1024:1536]), "vp")
            self.on("vector", v_st3)
            nvr = self.val("vp")

            def t_bc2(t, sp0=sp0, nvr=nvr):
                mk.w(t, "vp", nvr)
                t.matmul(mk.ps_a[:, sp0:sp0 + 512], mk.ones1_f[:, :],
                         mk.strow[0:1, 0:512], start=True, stop=True)
                mk.inc(t.matmul(mk.ps_b[:, sp0:sp0 + 512], mk.ones1_f[:, :],
                                mk.strow[0:1, 512:1024],
                                start=True, stop=True), "tp")
            self.on("tensor", t_bc2)
            tp_bc2 = self.val("tp")

        x2_free = [self.xstrip_free[0], self.xstrip_free[1]]
        vlast = None
        for k in range(KT):
            slot = k % 2

            def sy_fn(sy, k=k, slot=slot):
                sem, vv = x2_free[slot]
                sy.wait_ge(mk.sem(sem), vv)
                src = bass.AP(self.x_d, base + (128 * k) * (TH + S),
                              [[TH + S, 128], [1, TT]])
                mk.inc(sy.dma_start(
                    mk.xstrip[:, slot * 1024:slot * 1024 + TT], src),
                    ("d_x0", "d_xg")[slot], 16)
            self.on(("sync", "gpsimd")[slot], sy_fn)
            ndx = self.val(("d_x0", "d_xg")[slot])

            def v_ap(v, k=k, slot=slot, ndx=ndx):
                mk.w(v, ("d_x0", "d_xg")[slot], ndx)
                if k == 0:
                    mk.w(v, "tp", tp_bc2)
                sem, vv = mk.ostrip_free[0]
                v.wait_ge(mk.sem(sem), vv)
                for sp in range(nsp):
                    v.tensor_tensor(mk.tmp1[:, 0:512],
                                    mk.ao_v[:, k, sp * 512:sp * 512 + 512],
                                    mk.ps_a[:, sp * 512:sp * 512 + 512],
                                    ALU.subtract)
                    v.tensor_tensor(mk.tmp1[:, 0:512], mk.tmp1[:, 0:512],
                                    mk.ps_b[:, sp * 512:sp * 512 + 512],
                                    ALU.mult)
                    mk.inc(v.tensor_tensor(
                        mk.ostrip[:, sp * 512:sp * 512 + 512],
                        mk.tmp1[:, 0:512],
                        mk.xstrip[:, slot * 1024 + sp * 512:
                                  slot * 1024 + sp * 512 + 512],
                        ALU.add), "vp")
            self.on("vector", v_ap)
            nvv = self.val("vp")
            vlast = nvv

            def sy_x1(sy, k=k, slot=slot, nvv=nvv, tile=tile):
                mk.w(sy, "vp", nvv)
                dst = bass.AP(self.x1_d, tile * TT + (128 * k) * S,
                              [[S, 128], [1, TT]])
                mk.inc(sy.dma_start(dst, mk.ostrip[:, 0:TT]), "d_x1", 16)
            self.on("sync", sy_x1)
            x2_free[slot] = ("d_x1", self.val("d_x1"))
            mk.ostrip_free[0] = ("d_x1", self.val("d_x1"))
        self.xstrip_free = [x2_free[0], x2_free[1]]
        self.ps_ab_free = ("vp", vlast)

    def _ffn_tile(self, tile):
        mk = self
        TT, S = self.TT, self.S
        nsp = TT // 512
        ntb = TT // 128
        base = tile * TT

        if tile == 0:
            nall = self.val("d_x1")

            def sy_gate(sy):
                mk.w(sy, "d_x1", nall)
            self.on("sync", sy_gate)

        vp_y = self._ln_stream(self.x1_d, base, S, TT, self.y_v)

        G = 4
        rt_psfree = {}
        vp_prev = self.val("vp")
        sp_now = self.val("sp")
        tp_now = self.val("tp")
        vp_now = self.val("vp")
        chain_free = {j: ("sp", sp_now) for j in range(G)}
        cbf_free = {j: ("tp", tp_now) for j in range(G)}
        ctps_free = {j: ("vp", vp_now) for j in range(G)}
        sc = mk.sc_sb
        vsteps = [
            lambda v, B: v.tensor_tensor(sc[:, B + 8:B + 16],
                                         sc[:, B + 0:B + 8],
                                         mk.ebias_sb[:, :], ALU.add),
            lambda v, B: v.reduce_max(sc[:, B + 24:B + 25],
                                      sc[:, B + 16:B + 24], axis=AX.X),
            lambda v, B: v.tensor_scalar(sc[:, B + 26:B + 34],
                                         sc[:, B + 16:B + 24],
                                         sc[:, B + 25:B + 26], None,
                                         ALU.is_equal),
            lambda v, B: v.scalar_tensor_tensor(sc[:, B + 42:B + 50],
                                                sc[:, B + 34:B + 42], -1e30,
                                                sc[:, B + 16:B + 24],
                                                ALU.mult, ALU.add),
            lambda v, B: v.reduce_max(sc[:, B + 58:B + 59],
                                      sc[:, B + 50:B + 58], axis=AX.X),
            lambda v, B: v.tensor_scalar(sc[:, B + 60:B + 68],
                                         sc[:, B + 16:B + 24],
                                         sc[:, B + 59:B + 60], None,
                                         ALU.is_ge),
            lambda v, B: v.tensor_tensor(sc[:, B + 76:B + 84],
                                         sc[:, B + 0:B + 8],
                                         sc[:, B + 68:B + 76], ALU.mult),
            lambda v, B: v.reduce_sum(sc[:, B + 84:B + 85],
                                      sc[:, B + 85:B + 93], axis=AX.X),
            lambda v, B: v.tensor_scalar(sc[:, B + 94:B + 95],
                                         sc[:, B + 93:B + 94], 1e-9, None,
                                         ALU.add),
            lambda v, B: v.reciprocal(sc[:, B + 96:B + 97],
                                      sc[:, B + 95:B + 96]),
            lambda v, B: v.tensor_scalar(sc[:, B + 98:B + 106],
                                         sc[:, B + 85:B + 93],
                                         sc[:, B + 97:B + 98], None,
                                         ALU.mult),
        ]
        scopies = [[(16, 8, 8)], [(25, 24, 1)], [(34, 26, 8)],
                   [(50, 42, 8)], [(59, 58, 1)], [(68, 60, 8)],
                   [(85, 76, 8)], [(93, 84, 1)], [(95, 94, 1)],
                   [(97, 96, 1)], None]
        last_ct = None
        for g0 in range(0, ntb, G):
            tbs = list(range(g0, min(g0 + G, ntb)))
            pv = {}
            psv = {}
            for tb in tbs:
                pb = (tb % 2) * 512

                def t_rt(t, tb=tb, pb=pb):
                    mk.w(t, "vp", vp_y)
                    mk.w(t, "d_cst", mk.d_cst_all)
                    if pb in rt_psfree:
                        t.wait_ge(mk.sem(rt_psfree[pb][0]), rt_psfree[pb][1])
                    last = None
                    for k in range(KT):
                        last = t.matmul(
                            mk.ps_c[:, pb:pb + 8],
                            mk.y_v[:, k, tb * 128:(tb + 1) * 128],
                            mk.rwt_sb.ap().rearrange("p (k e) -> p k e",
                                                     k=KT)[:, k, :],
                            start=(k == 0), stop=(k == KT - 1))
                    mk.inc(last, "tp")
                self.on("tensor", t_rt)
                ntp = self.val("tp")

                def s_rt(s, tb=tb, pb=pb, ntp=ntp, cf=chain_free[tb % G]):
                    mk.w(s, "tp", ntp)
                    s.wait_ge(mk.sem(cf[0]), cf[1])
                    mk.inc(s.activation(
                        mk.sc_sb[:, (tb % G) * 128:(tb % G) * 128 + 8],
                        mk.ps_c[:, pb:pb + 8], AF.Sigmoid), "sp")
                self.on("scalar", s_rt)
                psv[tb] = self.val("sp")
                rt_psfree[pb] = ("sp", psv[tb])

            for step in range(len(vsteps)):
                for tb in tbs:
                    def v_f(v, tb=tb, step=step, w=psv[tb]):
                        mk.w(v, "sp", w)
                        mk.inc(vsteps[step](v, (tb % G) * 128), "vp")
                    mk.on("vector", v_f)
                    pv[tb] = self.val("vp")
                if scopies[step] is None:
                    continue
                for tb in tbs:
                    def s_f(s, tb=tb, step=step, w=pv[tb]):
                        mk.w(s, "vp", w)
                        B = (tb % G) * 128
                        last = None
                        for dst, srcc, n in scopies[step]:
                            last = s.copy(sc[:, B + dst:B + dst + n],
                                          sc[:, B + srcc:B + srcc + n])
                        mk.inc(last, "sp")
                    mk.on("scalar", s_f)
                    psv[tb] = self.val("sp")

            for tb in tbs:
                j = tb % G

                def s_cbf(s, tb=tb, j=j, w=pv[tb], pct=cbf_free[j]):
                    mk.w(s, "vp", w)
                    s.wait_ge(mk.sem(pct[0]), pct[1])
                    mk.inc(s.copy(mk.cbf[:, j * 8:j * 8 + 8],
                                  sc[:, (tb % G) * 128 + 98:
                                     (tb % G) * 128 + 106]), "sp")
                self.on("scalar", s_cbf)
                nvb = self.val("sp")
                chain_free[j] = ("sp", nvb)

                def t_ct(t, j=j, nvb=nvb, cpf=ctps_free[j]):
                    mk.w(t, "sp", nvb)
                    t.wait_ge(mk.sem(cpf[0]), cpf[1])
                    mk.inc(t.transpose(
                        mk.ps_tr[0:8, j * 128:j * 128 + 128],
                        mk.cbf[:, j * 8:j * 8 + 8],
                        mk.ident_bf[:, :]), "tp")
                self.on("tensor", t_ct)
                ntt = self.val("tp")
                cbf_free[j] = ("tp", ntt)

                def v_ct(v, tb=tb, j=j, ntt=ntt):
                    mk.w(v, "tp", ntt)
                    mk.inc(v.tensor_copy(
                        mk.cT_sb[:, tb * 128:(tb + 1) * 128],
                        mk.ps_tr[0:8, j * 128:j * 128 + 128]), "vp")
                self.on("vector", v_ct)
                last_ct = ("vp", self.val("vp"))
                ctps_free[j] = last_ct
            if tbs[-1] == ntb - 1 and tile == 0 and "t_sc" in self.tap_d:
                self.tap_bf("t_sc", self.sc_sb[:, 0:128])
        nct = self.val("vp")

        cb_psfree = {}
        crow_free = [("tp", 0), ("tp", 0)]
        for e in range(E):
            cs = e % 2

            def sy_cr(sy, e=e, cs=cs):
                if e < 2:
                    mk.w(sy, "vp", nct)
                sem, vv = crow_free[cs]
                sy.wait_ge(mk.sem(sem), vv)
                mk.inc(sy.dma_start(
                    mk.crow[0:1, cs * 1024:cs * 1024 + TT],
                    mk.cT_sb[e:e + 1, 0:TT]), "d_cr", 16)
            self.on("sync", sy_cr)
            ncr = self.val("d_cr")

            for sp in range(nsp):
                pb = ((e * nsp + sp) % 2) * 512

                def t_cb(t, e=e, cs=cs, sp=sp, pb=pb, ncr=ncr,
                         vp_prev=vp_prev):
                    mk.w(t, "d_cr", ncr)
                    mk.w(t, "vp", vp_prev)
                    if pb in cb_psfree:
                        t.wait_ge(mk.sem(cb_psfree[pb][0]), cb_psfree[pb][1])
                    mk.inc(t.matmul(
                        mk.ps_c[:, pb:pb + 512], mk.ones1_bf[:, :],
                        mk.crow[0:1, cs * 1024 + sp * 512:
                                cs * 1024 + sp * 512 + 512],
                        start=True, stop=True), "tp")
                self.on("tensor", t_cb)
                if sp == nsp - 1:
                    crow_free[cs] = ("tp", self.val("tp"))
                ntc = self.val("tp")

                def v_cb(v, e=e, sp=sp, pb=pb, ntc=ntc):
                    mk.w(v, "tp", ntc)
                    mk.inc(v.tensor_copy(
                        mk.cbc_v[:, e, sp * 512:sp * 512 + 512],
                        mk.ps_c[:, pb:pb + 512]), "vp")
                self.on("vector", v_cb)
                cb_psfree[pb] = ("vp", self.val("vp"))
        vp_cbc = self.val("vp")
        if tile == 0 and "t_cbc" in self.tap_d:
            self.tap_bf("t_cbc", self.cbc_v[:, :, 0:TT],
                        lambda dd: dd.ap().rearrange("p (k t) -> p k t", k=E))

        gu_free = {0: ("vp", vp_cbc), 1: mk.ps_ab_free}
        sq_free = {0: ("tp", 0), 1: ("tp", 0)}
        pstrip_free = self.vstrip_free
        for ht in range(HST):
            slot = ht % 2

            def g_w(g, ht=ht, slot=slot):
                sem, vv = mk.wfree[slot]
                g.wait_ge(mk.sem(sem), vv)
                src1 = bass.AP(self.sw1t_d, ht * 128,
                               [[HS, 128], [128 * HS, KT], [1, 128]])
                src2 = bass.AP(self.sw2t_d, ht * 128,
                               [[HS, 128], [128 * HS, KT], [1, 128]])
                mk.inc(g.dma_start(mk.wvv()[:, slot, :, 0:128], src1),
                       "d_w%d" % slot, 16)
                mk.inc(g.dma_start(mk.wvv()[:, slot, :, 128:256], src2),
                       "d_w%d" % slot, 16)
            self.on("gpsimd", g_w)
            nw = self.val("d_w%d" % slot)

            for sp in range(nsp):
                pset = (ht * nsp + sp) % 2
                if pset == 0:
                    gp_ap = mk.ps_c[:, 0:512]
                    up_ap = mk.ps_c[:, 512:1024]
                else:
                    gp_ap = mk.ps_a[:, 0:512]
                    up_ap = mk.ps_b[:, 0:512]

                def t_gu(t, ht=ht, slot=slot, sp=sp, nw=nw,
                         gp_ap=gp_ap, up_ap=up_ap, gu=gu_free[pset],
                         last_slot=(sp == nsp - 1)):
                    mk.w(t, "d_w%d" % slot, nw)
                    mk.w(t, "vp", vp_y)
                    t.wait_ge(mk.sem(gu[0]), gu[1])
                    last = None
                    for k in range(KT):
                        last = t.matmul(
                            gp_ap, mk.wvv()[:, slot, k, 0:128],
                            mk.y_v[:, k, sp * 512:sp * 512 + 512],
                            start=(k == 0), stop=(k == KT - 1))
                    for k in range(KT):
                        last = t.matmul(
                            up_ap, mk.wvv()[:, slot, k, 128:256],
                            mk.y_v[:, k, sp * 512:sp * 512 + 512],
                            start=(k == 0), stop=(k == KT - 1))
                    mk.inc(last, "tp")
                    if last_slot:
                        mk.wfree[slot] = ("tp", mk.cnt["tp"])
                self.on("tensor", t_gu)
                ntg = self.val("tp")

                def s_gu(s, ntg=ntg, gp_ap=gp_ap, pset=pset,
                         sqf=sq_free[pset]):
                    mk.w(s, "tp", ntg)
                    s.wait_ge(mk.sem(sqf[0]), sqf[1])
                    mk.inc(s.activation(
                        mk.sqk[:, pset * 512:pset * 512 + 512], gp_ap,
                        AF.Silu), "sp")
                self.on("scalar", s_gu)
                nsg = self.val("sp")
                ss = (ht * nsp + sp) % 2

                def v_gu(v, nsg=nsg, ss=ss, up_ap=up_ap, pset=pset):
                    mk.w(v, "sp", nsg)
                    sem, vv = pstrip_free[ss]
                    v.wait_ge(mk.sem(sem), vv)
                    mk.inc(v.tensor_tensor(
                        mk.xbfk[:, ss * 512:ss * 512 + 512],
                        mk.sqk[:, pset * 512:pset * 512 + 512], up_ap,
                        ALU.mult), "vp")
                self.on("vector", v_gu)
                gu_free[pset] = ("vp", self.val("vp"))
                sq_free[pset] = ("vp", self.val("vp"))
                nvp = self.val("vp")

                def sy_p(sy, ht=ht, sp=sp, ss=ss, nvp=nvp):
                    mk.w(sy, "vp", nvp)
                    dst = bass.AP(self.p_d, ht * 128 * S + base + sp * 512,
                                  [[S, 128], [1, 512]])
                    mk.inc(sy.dma_start(
                        dst, mk.xbfk[:, ss * 512:ss * 512 + 512]),
                        "d_p", 16)
                self.on("sync", sy_p)
                pstrip_free[ss] = ("d_p", self.val("d_p"))
                self.vstrip_free[ss] = pstrip_free[ss]
        self.ps_ab_free = ("vp", self.val("vp"))

        h12_free = {}
        w3_free = {}
        h12_free[("h12", 0)] = ("vp", self.val("vp"))
        h12_free[("h12", 512)] = ("vp", self.val("vp"))
        for eq in range(4):
            self._b5_quarter(tile, eq, vp_y, vp_cbc, h12_free, w3_free)
        self.ps_ab_free = ("vp", self.val("vp"))

    def _b5_quarter(self, tile, eq, vp_y, vp_cbc, h12_free, w3_free):
        mk = self
        TT, S = self.TT, self.S
        nsp = TT // 512
        base = tile * TT
        for e in (2 * eq, 2 * eq + 1):
            for ei in range(8):
                slot = ei % 2

                def g_w12(g, e=e, ei=ei, slot=slot):
                    sem, vv = mk.wfree[slot]
                    g.wait_ge(mk.sem(sem), vv)
                    src = bass.AP(self.w12t_d, e * D * 2 * HR + ei * 256,
                                  [[2 * HR, 128], [128 * 2 * HR, KT],
                                   [1, 256]])
                    mk.inc(g.dma_start(mk.wvv()[:, slot, :, :], src),
                           "d_w%d" % slot, 16)
                self.on("gpsimd", g_w12)
                nw = self.val("d_w%d" % slot)

                for ot in range(2):
                    habs = ei * 2 + ot
                    for sp in range(nsp):
                        pb = ((habs * nsp + sp) % 2) * 512

                        def t_h12(t, e=e, ei=ei, slot=slot, ot=ot, sp=sp,
                                  pb=pb, nw=nw,
                                  last_slot=(ot == 1 and sp == nsp - 1)):
                            mk.w(t, "d_w%d" % slot, nw)
                            mk.w(t, "vp", vp_y)
                            key = ("h12", pb)
                            if key in h12_free:
                                t.wait_ge(mk.sem(h12_free[key][0]),
                                          h12_free[key][1])
                            last = None
                            for k in range(KT):
                                last = t.matmul(
                                    mk.ps_c[:, pb:pb + 512],
                                    mk.wvv()[:, slot, k,
                                             ot * 128:(ot + 1) * 128],
                                    mk.y_v[:, k, sp * 512:sp * 512 + 512],
                                    start=(k == 0), stop=(k == KT - 1))
                            mk.inc(last, "tp")
                            if last_slot:
                                mk.wfree[slot] = ("tp", mk.cnt["tp"])
                        self.on("tensor", t_h12)
                        nth = self.val("tp")

                        if habs < 8:
                            def s_h1(s, e=e, habs=habs, sp=sp, pb=pb,
                                     nth=nth):
                                mk.w(s, "tp", nth)
                                mk.inc(s.activation(
                                    mk.s1_v[:, (e % 2) * 8 + habs,
                                            sp * 512:sp * 512 + 512],
                                    mk.ps_c[:, pb:pb + 512], AF.Silu), "sp")
                            self.on("scalar", s_h1)
                            h12_free[("h12", pb)] = ("sp", self.val("sp"))
                        else:
                            hti = habs - 8

                            def v_h2(v, e=e, hti=hti, sp=sp, pb=pb, nth=nth):
                                mk.w(v, "tp", nth)
                                mk.w(v, "vp", vp_cbc)
                                v.tensor_tensor(
                                    mk.tmp1[:, 0:512],
                                    mk.s1_v[:, (e % 2) * 8 + hti,
                                            sp * 512:sp * 512 + 512],
                                    mk.ps_c[:, pb:pb + 512], ALU.mult)
                                mk.inc(v.tensor_tensor(
                                    mk.hq_v[:, (e % 2) * 8 + hti,
                                            sp * 512:sp * 512 + 512],
                                    mk.tmp1[:, 0:512],
                                    mk.cbc_v[:, e, sp * 512:sp * 512 + 512],
                                    ALU.mult), "vp")
                            self.on("vector", v_h2)
                            h12_free[("h12", pb)] = ("vp", self.val("vp"))
        vp_hq = self.val("vp")

        for de in range(8):
            nws = []
            for ie, e in enumerate((2 * eq, 2 * eq + 1)):
                slot = ie

                def g_w3(g, e=e, de=de, slot=slot):
                    sem, vv = mk.wfree[slot]
                    g.wait_ge(mk.sem(sem), vv)
                    src = bass.AP(self.w3t_d, e * HR * D + de * 256,
                                  [[D, 128], [128 * D, 8], [1, 256]])
                    mk.inc(g.dma_start(mk.wvv()[:, slot, 0:8, :], src),
                           "d_w%d" % slot, 16)
                self.on("gpsimd", g_w3)
                nws.append(self.val("d_w%d" % slot))

            for sp in range(nsp):
                pst = mk.ps_a if (de % 2) == 0 else mk.ps_b
                key = ("w3", de % 2)

                def t_w3(t, eq=eq, de=de, sp=sp, pst=pst, nws=tuple(nws),
                         key=key, vp_hq=vp_hq):
                    mk.w(t, "d_w0", nws[0])
                    mk.w(t, "d_w1", nws[1])
                    mk.w(t, "vp", vp_hq)
                    if key in w3_free:
                        t.wait_ge(mk.sem(w3_free[key][0]), w3_free[key][1])
                    last = None
                    for dt in range(2):
                        for ie in range(2):
                            for ht in range(8):
                                last = t.matmul(
                                    pst[:, dt * 512:dt * 512 + 512],
                                    mk.wvv()[:, ie, ht,
                                             dt * 128:dt * 128 + 128],
                                    mk.hq_v[:, ie * 8 + ht,
                                            sp * 512:sp * 512 + 512],
                                    start=(ie == 0 and ht == 0),
                                    stop=(ie == 1 and ht == 7))
                        mk.inc(last, "tp")
                    if sp == nsp - 1:
                        mk.wfree[0] = ("tp", mk.cnt["tp"])
                        mk.wfree[1] = ("tp", mk.cnt["tp"])
                self.on("tensor", t_w3)
                ntw = self.val("tp")

                def v_w3(v, pst=pst, ntw=ntw, of0=self.ostrip_free[0]):
                    mk.w(v, "tp", ntw)
                    v.wait_ge(mk.sem(of0[0]), of0[1])
                    v.tensor_copy(mk.ostrip[:, 0:512], pst[:, 0:512])
                    mk.inc(v.tensor_copy(mk.ostrip[:, 512:1024],
                                         pst[:, 512:1024]), "vp")
                self.on("vector", v_w3)
                w3_free[key] = ("vp", self.val("vp"))
                nvw = self.val("vp")

                def g_fa(g, eq=eq, de=de, sp=sp, nvw=nvw, base=base):
                    mk.w(g, "vp", nvw)
                    dst = bass.AP(self.ffn_d, de * 256 * S + base + sp * 512,
                                  [[S, 128], [128 * S, 2], [1, 512]])
                    src = mk.ostrip.ap().rearrange(
                        "p (i t) -> p i t", i=2)[:, :, 0:512]
                    mk.inc(g.dma_start(
                        dst, src,
                        accum_op=(ALU.bypass if eq == 0 else ALU.add)),
                        "d_fa", 16)
                self.on("gpsimd", g_fa)
                self.ostrip_free[0] = ("d_fa", self.val("d_fa"))

    def _b4(self):
        mk = self
        S = self.S
        nts = S // 256
        tp_b5 = self.val("tp")
        vp_b5 = self.val("vp")
        ndp = self.val("d_p")

        def sy_gate(sy):
            mk.w(sy, "d_p", ndp)
        self.on("sync", sy_gate)
        p_free = [("tp", tp_b5), ("tp", tp_b5)]
        b4_ps = None
        for dq in range(4):
            tp_now = self.val("tp")

            def g_sw3(g, dq=dq, tp_now=tp_now):
                g.wait_ge(mk.sem("tp"), tp_now)
                if dq == 0 and "d_tap" in mk.sems:
                    mk.w(g, "d_tap")
                src_a = bass.AP(self.sw3t_d, dq * 512,
                                [[D, 128], [128 * D, 32], [1, 512]])
                src_b = bass.AP(self.sw3t_d, 32 * 128 * D + dq * 512,
                                [[D, 128], [128 * D, 32], [1, 512]])
                mk.inc(g.dma_start(mk.sw3a_v[:, :, 0:512], src_a),
                       "d_s3", 16)
                mk.inc(g.dma_start(mk.sw3b_v[:, :, 0:512], src_b),
                       "d_s3", 16)
            self.on("gpsimd", g_sw3)
            sw_need = self.val("d_s3")

            for ts in range(nts):
                slot = ts % 2

                def sy_pl(sy, ts=ts, slot=slot):
                    sem, vv = p_free[slot]
                    sy.wait_ge(mk.sem(sem), vv)
                    src = bass.AP(self.p_d, ts * 256,
                                  [[S, 128], [128 * S, HST], [1, 256]])
                    mk.inc(sy.dma_start(mk.pb4_v[slot][:, :, 0:256], src),
                           "d_pb%d" % slot, 16)
                self.on("sync", sy_pl)
                npb = self.val("d_pb%d" % slot)

                def t_b4(t, dq=dq, ts=ts, slot=slot, npb=npb,
                         sw_need=sw_need, b4_ps=b4_ps, vp_b5=vp_b5):
                    mk.w(t, "d_pb%d" % slot, npb)
                    mk.w(t, "d_s3", sw_need)
                    if dq == 0 and ts == 0:
                        mk.w(t, "vp", vp_b5)
                    if b4_ps is not None:
                        t.wait_ge(mk.sem(b4_ps[0]), b4_ps[1])
                    for dt in range(4):
                        pst = (mk.ps_a if dt < 2 else mk.ps_b)
                        pb = (dt % 2) * 512
                        last = None
                        for ht in range(HST):
                            lhs = (mk.sw3a_v if ht < 32 else mk.sw3b_v)[
                                :, ht % 32, dt * 128:(dt + 1) * 128]
                            last = t.matmul(
                                pst[:, pb:pb + 256], lhs,
                                mk.pb4_v[slot][:, ht, 0:256],
                                start=(ht == 0), stop=(ht == HST - 1))
                        mk.inc(last, "tp")
                    p_free[slot] = ("tp", mk.cnt["tp"])
                self.on("tensor", t_b4)
                ntb4 = self.val("tp")

                def v_b4(v, ntb4=ntb4, of0=self.ostrip_free[0]):
                    mk.w(v, "tp", ntb4)
                    v.wait_ge(mk.sem(of0[0]), of0[1])
                    for dt in range(4):
                        pst = (mk.ps_a if dt < 2 else mk.ps_b)
                        pb = (dt % 2) * 512
                        v.tensor_copy(mk.ostrip[:, dt * 256:dt * 256 + 256],
                                      pst[:, pb:pb + 256])
                    mk.inc(v.engine_nop(), "vp")
                self.on("vector", v_b4)
                b4_ps = ("vp", self.val("vp"))
                nvb = self.val("vp")

                def g_fa4(g, dq=dq, ts=ts, nvb=nvb):
                    mk.w(g, "vp", nvb)
                    dst = bass.AP(self.ffn_d, dq * 512 * S + ts * 256,
                                  [[S, 128], [128 * S, 4], [1, 256]])
                    src = mk.ostrip.ap().rearrange(
                        "p (i t) -> p i t", i=4)[:, :, 0:256]
                    mk.inc(g.dma_start(dst, src, accum_op=ALU.add),
                           "d_fa", 16)
                self.on("gpsimd", g_fa4)
                self.ostrip_free[0] = ("d_fa", self.val("d_fa"))
        self.ps_ab_free = ("vp", self.val("vp"))

    def _b6(self):
        mk = self
        S, TT = self.S, self.TT
        dfa = self.val("d_fa")

        def sy_gate(sy):
            mk.w(sy, "d_fa", dfa)
        self.on("sync", sy_gate)
        if "t_x1" in self.tap_d:
            def g_tx(g, dfa=dfa, dx1=self.val("d_x1")):
                mk.w(g, "d_x1", dx1)
                mk.inc(g.dma_start(self.tap_d["t_x1"].ap(), self.x1_d.ap()),
                       "d_tap", 16)
            self.on("gpsimd", g_tx)
        if "t_ffn" in self.tap_d:
            def g_tf(g, dfa=dfa):
                mk.w(g, "d_fa", dfa)
                mk.inc(g.dma_start(self.tap_d["t_ffn"].ap(), self.ffn_d.ap()),
                       "d_tap", 16)
            self.on("gpsimd", g_tf)
        for tile in range(S // TT):
            base = tile * TT
            nsp = TT // 512

            vp_ln = self._ln_stream(self.ffn_d, base, S, TT, self.xln_v)
            x2_free = [self.xstrip_free[0], self.xstrip_free[1]]
            for k in range(KT):
                slot = k % 2

                def sy_f(sy, k=k, slot=slot, base=base):
                    sem, vv = x2_free[slot]
                    sy.wait_ge(mk.sem(sem), vv)
                    src = bass.AP(self.ffn_d, base + (128 * k) * S,
                                  [[S, 128], [1, TT]])
                    mk.inc(sy.dma_start(
                        mk.xstrip[:, slot * 1024:slot * 1024 + TT], src),
                        ("d_x0", "d_xg")[slot], 16)
                self.on(("sync", "gpsimd")[slot], sy_f)
                ndf = self.val(("d_x0", "d_xg")[slot])

                def sy_x1(sy, k=k, slot=slot, base=base,
                          osf=self.ostrip_free[0]):
                    sy.wait_ge(mk.sem(osf[0]), osf[1])
                    src = bass.AP(self.x1_d, base + (128 * k) * S,
                                  [[S, 128], [1, TT]])
                    mk.inc(sy.dma_start(mk.ostrip[:, 0:TT], src),
                           "d_r%d" % slot, 16)
                self.on(("sync", "gpsimd")[slot], sy_x1)
                ndr = self.val("d_r%d" % slot)

                def v_ap(v, k=k, slot=slot, ndf=ndf, ndr=ndr):
                    mk.w(v, ("d_x0", "d_xg")[slot], ndf)
                    mk.w(v, "d_r%d" % slot, ndr)
                    for sp in range(nsp):
                        v.tensor_tensor(
                            mk.tmp1[:, 0:512],
                            mk.xstrip[:, slot * 1024 + sp * 512:
                                      slot * 1024 + sp * 512 + 512],
                            mk.ps_a[:, sp * 512:sp * 512 + 512],
                            ALU.subtract)
                        v.tensor_tensor(mk.tmp1[:, 0:512], mk.tmp1[:, 0:512],
                                        mk.ps_b[:, sp * 512:sp * 512 + 512],
                                        ALU.mult)
                        mk.inc(v.tensor_tensor(
                            mk.ostrip[:, sp * 512:sp * 512 + 512],
                            mk.tmp1[:, 0:512],
                            mk.ostrip[:, sp * 512:sp * 512 + 512],
                            ALU.add), "vp")
                self.on("vector", v_ap)
                nva = self.val("vp")

                def sy_o(sy, k=k, slot=slot, nva=nva, base=base):
                    mk.w(sy, "vp", nva)
                    dst = bass.AP(self.out_d, base + (128 * k) * S,
                                  [[S, 128], [1, TT]])
                    mk.inc(sy.dma_start(dst, mk.ostrip[:, 0:TT]), "d_out", 16)
                self.on("sync", sy_o)
                x2_free[slot] = ("d_out", self.val("d_out"))
                self.xstrip_free[slot] = ("vp", nva)
                self.ostrip_free[0] = ("d_out", self.val("d_out"))
            self.ps_ab_free = ("vp", self.val("vp"))

    def _finish(self):
        mk = self
        nout = self.val("d_out")

        def sy_end(sy):
            mk.w(sy, "d_out", nout)
        self.on("sync", sy_end)


def kernel(L0_attn_x, L0_attn_Wq, L0_attn_Wf, L0_attn_Wi, L0_attn_Wo,
           L0_attn_g_weight, L0_ffn_w12, L0_ffn_w3, L0_ffn_router_w,
           L0_ffn_expert_bias, L0_ffn_sw1, L0_ffn_sw2, L0_ffn_sw3):
    S_CORE, TH = 4096, 512
    x = np.asarray(L0_attn_x, dtype=np.float32)
    B, SEQ, _ = x.shape
    NQ = SEQ // S_CORE

    mk = MK(S_CORE=S_CORE, TH=TH, TT=1024)
    nc = mk.build()

    shared = {
        "wq": np.ascontiguousarray(np.asarray(L0_attn_Wq, np.float32)),
        "wf": np.ascontiguousarray(np.asarray(L0_attn_Wf, np.float32)),
        "wi": np.ascontiguousarray(np.asarray(L0_attn_Wi, np.float32)),
        "wo": np.ascontiguousarray(np.asarray(L0_attn_Wo, np.float32)),
        "gwrow": np.ascontiguousarray(
            np.asarray(L0_attn_g_weight, np.float32)[None, :]),
        "sw1t": np.ascontiguousarray(np.asarray(L0_ffn_sw1, np.float32).T),
        "sw2t": np.ascontiguousarray(np.asarray(L0_ffn_sw2, np.float32).T),
        "sw3t": np.ascontiguousarray(np.asarray(L0_ffn_sw3, np.float32).T),
        "w12t": np.ascontiguousarray(
            np.asarray(L0_ffn_w12, np.float32).transpose(0, 2, 1)),
        "w3t": np.ascontiguousarray(
            np.asarray(L0_ffn_w3, np.float32).transpose(0, 2, 1)),
        "rwt": np.ascontiguousarray(
            np.asarray(L0_ffn_router_w, np.float32).T),
        "ebias": np.ascontiguousarray(np.broadcast_to(
            np.asarray(L0_ffn_expert_bias, np.float32)[None, :], (128, E))),
    }

    in_maps = []
    for c in range(8):
        b, q = c // NQ, c % NQ
        main = x[b, q * S_CORE:(q + 1) * S_CORE, :]
        if q > 0:
            halo = x[b, q * S_CORE - TH:q * S_CORE, :]
            hon = np.ones((128, 1), np.float32)
        else:
            halo = np.zeros((TH, D), np.float32)
            hon = np.zeros((128, 1), np.float32)
        xt = np.ascontiguousarray(
            np.concatenate([halo, main], axis=0).T)
        m = dict(shared)
        m["x"] = xt
        m["halo_on"] = hon
        in_maps.append(m)

    res = run_bass_kernel_spmd(nc, in_maps, core_ids=list(range(8)))

    out = np.empty((B, SEQ, D), np.float32)
    for c in range(8):
        b, q = c // NQ, c % NQ
        out[b, q * S_CORE:(q + 1) * S_CORE, :] = res.results[c]["out"].T
    return out


# version 07 (final)
# speedup vs baseline: 1.1504x; 1.1504x over previous
import math

import numpy as np

import concourse.bass as bass
import concourse.mybir as mybir
from concourse.bass_utils import run_bass_kernel_spmd

F32 = mybir.dt.float32
BF16 = mybir.dt.bfloat16
AF = mybir.ActivationFunctionType
ALU = mybir.AluOpType
AX = mybir.AxisListType

D = 2048
KT = 16
NH = 16
DK = 128
E = 8
HR = 1024
HS = 8192
HST = HS // 128
C = 64
LN_EPS = 1e-5
RMS_EPS = 1e-5
CLAMP = 85.0


class MK:
    def __init__(self, S_CORE=4096, TH=512, TT=1024, taps=()):
        assert S_CORE % TT == 0 and TH % C == 0 and TT % 512 == 0
        assert TH <= TT <= 1024
        self.S = S_CORE
        self.TH = TH
        self.TT = TT
        self.NTT = S_CORE // TT
        self.NCH = TT // C
        self.taps = list(taps)
        self.nc = bass.Bass("TRN2", target_bir_lowering=False,
                            detect_race_conditions=False)
        self.cnt = {}
        self.sems = {}
        self.wfree = [("tp", 0), ("tp", 0)]
        self.xstrip_free = [("tp", 0), ("tp", 0)]
        self.vstrip_free = [("tp", 0), ("tp", 0)]
        self.ostrip_free = [("tp", 0), ("tp", 0)]

    def sem(self, name):
        if name not in self.sems:
            self.sems[name] = self.es.enter_context(self.nc.semaphore(name))
            self.cnt[name] = 0
        return self.sems[name]

    def inc(self, instr, name, k=1):
        instr.then_inc(self.sem(name), k)
        self.cnt[name] += k
        return self.cnt[name]

    def val(self, name):
        self.sem(name)
        return self.cnt[name]

    def w(self, eng, name, v=None):
        eng.wait_ge(self.sem(name), self.cnt[name] if v is None else v)

    def on(self, engname, fn):
        getattr(self.block, engname)(fn)

    def tap_bf(self, name, ap, dst_ap_fn=None):
        if name not in self.tap_d:
            return
        mk = self
        nv = self.val("vp")
        ns = self.val("sp")
        nt = self.val("tp")
        dst = self.tap_d[name].ap() if dst_ap_fn is None else \
            dst_ap_fn(self.tap_d[name])

        def sy_t(g):
            mk.w(g, "vp", nv)
            mk.w(g, "sp", ns)
            mk.w(g, "tp", nt)
            mk.inc(g.dma_start(dst, ap), "d_tap", 16)
        self.on("gpsimd", sy_t)

    def fence(self):
        tp, vp, sp = self.val("tp"), self.val("vp"), self.val("sp")

        def t_f(t):
            t.wait_ge(self.sem("vp"), vp)
            t.wait_ge(self.sem("sp"), sp)
        self.on("tensor", t_f)

        def v_f(v):
            v.wait_ge(self.sem("tp"), tp)
            v.wait_ge(self.sem("sp"), sp)
        self.on("vector", v_f)

        def s_f(s):
            s.wait_ge(self.sem("tp"), tp)
            s.wait_ge(self.sem("vp"), vp)
        self.on("scalar", s_f)

    def build(self):
        nc = self.nc
        S, TH, TT = self.S, self.TH, self.TT
        self.x_d = nc.dram_tensor("x", [D, TH + S], F32, kind="ExternalInput")
        self.wdr = {n: nc.dram_tensor(n, [8 * 128, 4096], F32,
                                      kind="ExternalInput")
                    for n in ("wq", "wf", "wi", "wo")}
        self.gwrow_d = nc.dram_tensor("gwrow", [1, D], F32,
                                      kind="ExternalInput")
        self.sw12_d = nc.dram_tensor("sw12", [64 * 128, 4096], F32,
                                     kind="ExternalInput")
        self.sw3t_d = nc.dram_tensor("sw3t", [8 * 128, 16384], F32,
                                     kind="ExternalInput")
        self.w12t_d = nc.dram_tensor("w12t", [E * 8 * 128, 4096], F32,
                                     kind="ExternalInput")
        self.w3t_d = nc.dram_tensor("w3t", [E * 8 * 128, 2048], F32,
                                    kind="ExternalInput")
        self.rwt_d = nc.dram_tensor("rwt", [D, E], F32, kind="ExternalInput")
        self.ebias_d = nc.dram_tensor("ebias", [128, E], F32,
                                      kind="ExternalInput")
        self.halo_d = nc.dram_tensor("halo_on", [128, 1], F32,
                                     kind="ExternalInput")
        self.out_d = nc.dram_tensor("out", [D, S], F32, kind="ExternalOutput")
        self.x1_d = nc.dram_tensor("x1s", [D, S], F32)
        self.vb_d = nc.dram_tensor("vbounce", [TT, D], BF16)
        self.p_d = nc.dram_tensor("pscratch", [HS, S], BF16)
        self.ffn_d = nc.dram_tensor("ffnacc", [D, S], F32)
        self.tap_d = {}
        for name, shape in self.taps:
            self.tap_d[name] = nc.dram_tensor(name, list(shape), F32,
                                              kind="ExternalOutput")

        from contextlib import ExitStack
        self.es = ExitStack()
        with self.es:
            self._alloc()
            self.block = self.es.enter_context(nc.Block())
            self._emit()
        return nc

    def _alloc(self):
        nc, es = self.nc, self.es
        TT = self.TT
        sb = lambda n, s, d: es.enter_context(nc.sbuf_tensor(n, s, d))
        ps = lambda n, s, d: es.enter_context(nc.psum_tensor(n, s, d))
        self.ident_bf = sb("ident_bf", [128, 128], BF16)
        self.maskr = sb("maskr", [C, 8 * C], BF16)
        self.cbf = sb("cbf", [128, 32], BF16)
        self.ones_bf = sb("ones_bf", [128, 1], BF16)
        self.ones1_f = sb("ones1_f", [1, 128], F32)
        self.c_eps = sb("c_eps", [128, 1], F32)
        self.c_reps = sb("c_reps", [128, 1], F32)
        self.c_invd = sb("c_invd", [128, 1], F32)
        self.c_lnsc = sb("c_lnsc", [128, 1], F32)
        self.zeros_c = sb("zeros_c", [128, C], F32)
        self.gw_rep = sb("gw_rep", [128, D], BF16)
        self.ebias_sb = sb("ebias_sb", [128, E], F32)
        self.halo_sb = sb("halo_sb", [128, 1], F32)
        self.rwt_sb = sb("rwt_sb", [128, KT * E], BF16)
        self.h_st = sb("h_st", [128, NH * DK], F32)
        self.h_bf = sb("h_bf", [128, NH * DK], BF16)
        self.ech = sb("ech", [128, NH * self.NCH], F32)
        self.am_sb = sb("am_sb", [C, 8 * C], BF16)
        self.ktm_sb = sb("ktm_sb", [C, 8 * DK], BF16)
        self.otmp = sb("otmp", [C, D], BF16)
        self.rms_sc = sb("rms_sc", [C, 4], F32)
        self.dbg1 = sb("dbg1", [128, 64], BF16)
        self.dbg2 = sb("dbg2", [64, 128], BF16)
        self.tmp1 = sb("tmp1", [128, TT], F32)
        self.strow = sb("strow", [1, 1536], F32)
        self.tmp2 = None
        self.tmp3 = None
        self.xbfk = sb("xbfk", [128, 1024], BF16)
        self.sqk = sb("sqk", [128, 1024], BF16)
        self.xstrip = sb("xstrip", [128, 2048], F32)
        self.ostrip = sb("ostrip", [128, 1024], F32)
        self.tmp2 = self.xstrip[:, 0:1024]
        self.tmp3 = self.xstrip[:, 1024:2048]
        self.sc_sb = sb("sc_sb", [128, 512], F32)
        self.crow = sb("crow", [1, 2 * 1024], BF16)
        self.cT_sb = sb("cT_sb", [8, 1024], BF16)
        self.ones1_bf = sb("ones1_bf", [1, 128], BF16)
        self.P1 = sb("P1", [128, 8192], F32)
        self.P2 = sb("P2", [128, 8192], F32)
        self.P3 = sb("P3", [128, 8192], F32)
        self.P4 = sb("P4", [128, 8192], F32)
        self.WP = sb("WP", [128, 4096], F32)
        self.ps_a = ps("ps_a", [128, 1024], F32)
        self.ps_b = ps("ps_b", [128, 1024], F32)
        self.ps_c = ps("ps_c", [128, 1024], F32)
        self.ps_tr2 = ps("ps_tr2", [64, 1024], BF16)
        self.ps_tr = ps("ps_tr", [128, 1024], BF16)

        self.xln_v = self.P1.ap().bitcast(BF16).rearrange(
            "p (k t) -> p k t", k=KT)
        self.ofm_v = self.P1.ap().bitcast(BF16).rearrange(
            "p (k t) -> p k t", k=KT)
        self.qt_v = self.P2.ap().bitcast(BF16).rearrange(
            "p (h t) -> p h t", h=NH)
        self.ao_v = self.P2.ap().bitcast(BF16).rearrange(
            "p (k t) -> p k t", k=KT)
        self.kt_v = self.P3.ap().bitcast(BF16).rearrange(
            "p (h t) -> p h t", h=NH)
        self.v2_v = self.P4[0:C, :].bitcast(BF16).rearrange(
            "p (c d) -> p c d", d=D)
        self.y_v = self.xln_v
        self.cbc_v = self.P2.ap().rearrange("p (e t) -> p e t", e=E)
        self.hq_v = self.P3.ap().bitcast(BF16).rearrange(
            "p (i t) -> p i t", i=16)
        self.s1_v = self.P4.ap().bitcast(BF16).rearrange(
            "p (i t) -> p i t", i=16)
        self.wv = self.WP.ap().bitcast(BF16)
        self.sw3a_v = self.P1.ap().bitcast(BF16).rearrange(
            "p (h t) -> p h t", h=32)
        self.sw3b_v = self.P2.ap().bitcast(BF16).rearrange(
            "p (h t) -> p h t", h=32)
        self.pb4_v = [self.P3.ap().bitcast(BF16).rearrange(
            "p (h t) -> p h t", h=HST),
            self.P4.ap().bitcast(BF16).rearrange(
                "p (h t) -> p h t", h=HST)]

    def wvv(self):
        return self.wv.rearrange("p (s k w) -> p s k w", s=2, k=KT)

    def _emit(self):
        self._consts()
        self._attn_tile(is_halo=True, tile=0)
        for t in range(self.NTT):
            self.fence()
            self._attn_tile(is_halo=False, tile=t)
        for t in range(self.NTT):
            self.fence()
            self._ffn_tile(t)
        self.fence()
        self._b4()
        self.fence()
        self._b6()
        self._finish()

    def _consts(self):
        mk = self

        def g_fn(g):
            g.memset(mk.ident_bf[:, :], 0.0)
            g.memset(mk.maskr[:, :], 1.0)
            g.memset(mk.ones_bf[:, :], 1.0)
            g.memset(mk.ones1_f[:, :], 1.0)
            g.memset(mk.ones1_bf[:, :], 1.0)
            g.memset(mk.c_eps[:, :], LN_EPS)
            g.memset(mk.c_reps[:, :], RMS_EPS)
            g.memset(mk.c_invd[:, :], 1.0 / D)
            g.memset(mk.c_lnsc[:, :], -0.5 * math.log(DK))
            g.memset(mk.zeros_c[:, :], 0.0)
            g.memset(mk.h_st[:, :], 0.0)
            mk.inc(g.memset(mk.h_bf[:, :], 0.0), "gp")
            mk.w(g, "gp")
            g.affine_select(out=mk.ident_bf[:, :], in_=mk.ident_bf[:, :],
                            compare_op=ALU.not_equal, fill=1.0, base=0,
                            pattern=[[-1, 128]], channel_multiplier=1)
            mk.inc(g.affine_select(
                out=mk.maskr.ap().rearrange("p (b t) -> p b t", b=8),
                in_=mk.maskr.ap().rearrange("p (b t) -> p b t", b=8),
                compare_op=ALU.is_ge, fill=0.0, base=0,
                pattern=[[0, 8], [1, C]], channel_multiplier=-1), "gp")
            mk.inc(g.dma_start(
                mk.rwt_sb.ap().rearrange("p (k e) -> p k e", k=KT),
                mk.rwt_d.ap().rearrange("(k p) e -> p k e", p=128)),
                "d_cst", 16)
            mk.inc(g.dma_start(mk.ebias_sb[:, :], mk.ebias_d[:, :]),
                   "d_cst", 16)
            mk.inc(g.dma_start(mk.halo_sb[:, :], mk.halo_d[:, :]),
                   "d_cst", 16)
        self.on("gpsimd", g_fn)
        self.gp_consts = self.val("gp")
        self.d_cst_all = self.val("d_cst")

        def sy_fn(sy):
            mk.inc(sy.dma_start(mk.xstrip[0:1, 0:D], mk.gwrow_d[0:1, :]),
                   "d_gw", 16)
        self.on("sync", sy_fn)

        def t_fn(t):
            mk.w(t, "d_gw", 16)
            mk.w(t, "gp", mk.gp_consts)
            for n in range(4):
                dst = (mk.ps_a if n < 2 else mk.ps_b)
                mk.inc(t.matmul(dst[:, (n % 2) * 512:(n % 2) * 512 + 512],
                                mk.ones1_f[:, :],
                                mk.xstrip[0:1, n * 512:(n + 1) * 512],
                                start=True, stop=True), "tp")
        self.on("tensor", t_fn)
        tpv = self.val("tp")

        def v_fn(v):
            mk.w(v, "tp", tpv)
            v.tensor_copy(mk.gw_rep[:, 0:1024], mk.ps_a[:, :])
            mk.inc(v.tensor_copy(mk.gw_rep[:, 1024:2048], mk.ps_b[:, :]),
                   "vp")
        self.on("vector", v_fn)
        self.ps_ab_free = ("vp", self.val("vp"))
        self.xstrip_free = [self.ps_ab_free, self.ps_ab_free]

    def _ln_stream(self, src_dram, base, stride, tlen, dst_v):
        mk = self
        nsp = (tlen + 511) // 512
        spans = [(i * 512, min(512, tlen - i * 512)) for i in range(nsp)]
        st_free = [self.xstrip_free[0], self.xstrip_free[1]]
        qsem = ["d_x0", "d_xg"]
        qeng = ["sync", "gpsimd"]
        tmp1bf = self.tmp1.ap().bitcast(BF16)
        xb = [mk.xbfk, tmp1bf[:, 0:1024]]
        sq = [mk.sqk, tmp1bf[:, 1024:2048]]
        xs_free = [[st_free[0]], [st_free[1]]]
        stat_free = [st_free[0], st_free[1]]
        vp_tmp1 = self.val("vp")
        for k in range(KT):
            slot = k % 2

            def sy_fn(sy, k=k, slot=slot, xf=tuple(xs_free[slot])):
                for sem, vv in xf:
                    sy.wait_ge(mk.sem(sem), vv)
                src = bass.AP(src_dram, base + (128 * k) * stride,
                              [[stride, 128], [1, tlen]])
                mk.inc(sy.dma_start(
                    mk.xstrip[:, slot * 1024:slot * 1024 + tlen], src),
                    qsem[slot], 16)
            self.on(qeng[slot], sy_fn)
            ndx = self.val(qsem[slot])

            def v_fn(v, k=k, slot=slot, ndx=ndx, stf=tuple(stat_free[slot])):
                mk.w(v, qsem[slot], ndx)
                v.wait_ge(mk.sem(stf[0]), stf[1])
                if k < 2:
                    for sem2, vv2 in (self.vstrip_free[0],
                                      self.vstrip_free[1]):
                        v.wait_ge(mk.sem(sem2), vv2)
                mk.inc(v.tensor_copy(
                    xb[slot][:, 0:tlen],
                    mk.xstrip[:, slot * 1024:slot * 1024 + tlen]), "vp")
            self.on("vector", v_fn)
            nv = self.val("vp")

            def s_fn(s, k=k, slot=slot, ndx=ndx, stf=tuple(stat_free[slot])):
                mk.w(s, qsem[slot], ndx)
                s.wait_ge(mk.sem(stf[0]), stf[1])
                if k < 2:
                    mk.w(s, "vp", vp_tmp1)
                mk.inc(s.activation(
                    sq[slot][:, 0:tlen],
                    mk.xstrip[:, slot * 1024:slot * 1024 + tlen],
                    AF.Square), "sp")
            self.on("scalar", s_fn)
            ns = self.val("sp")

            def t_fn(t, k=k, slot=slot, nv=nv, ns=ns):
                mk.w(t, "vp", nv)
                mk.w(t, "sp", ns)
                if k == 0:
                    mk.w(t, *mk.ps_ab_free)
                last = None
                for sp0, sl in spans:
                    last = t.matmul(
                        mk.ps_a[0:1, sp0:sp0 + sl], mk.ones_bf[:, :],
                        xb[slot][:, sp0:sp0 + sl],
                        start=(k == 0), stop=(k == KT - 1))
                    last = t.matmul(
                        mk.ps_b[0:1, sp0:sp0 + sl], mk.ones_bf[:, :],
                        sq[slot][:, sp0:sp0 + sl],
                        start=(k == 0), stop=(k == KT - 1))
                mk.inc(last, "tp")
            self.on("tensor", t_fn)
            stat_free[slot] = ("tp", self.val("tp"))
            xs_free[slot] = [("vp", nv), ("sp", ns)]
        tp_stats = self.val("tp")

        tp_bc = None
        for sp0, sl in spans:
            def v_fn2(v, sp0=sp0, sl=sl, tp_prev=tp_bc):
                mk.w(v, "tp", tp_stats)
                if tp_prev is not None:
                    v.wait_ge(mk.sem("tp"), tp_prev)
                v.tensor_scalar(mk.strow[0:1, 0:sl], mk.ps_a[0:1, sp0:sp0 + sl],
                                1.0 / D, None, ALU.mult)
                v.tensor_scalar(mk.strow[0:1, 512:512 + sl],
                                mk.ps_b[0:1, sp0:sp0 + sl], 1.0 / D, None,
                                ALU.mult)
                v.tensor_tensor(mk.strow[0:1, 1024:1024 + sl],
                                mk.strow[0:1, 0:sl], mk.strow[0:1, 0:sl],
                                ALU.mult)
                mk.inc(v.tensor_tensor(mk.strow[0:1, 512:512 + sl],
                                       mk.strow[0:1, 512:512 + sl],
                                       mk.strow[0:1, 1024:1024 + sl],
                                       ALU.subtract), "vp")
            self.on("vector", v_fn2)
            nvv = self.val("vp")

            def s_fn2(s, sl=sl, nvv=nvv):
                mk.w(s, "vp", nvv)
                mk.inc(s.activation(mk.strow[0:1, 1024:1024 + sl],
                                    mk.strow[0:1, 512:512 + sl], AF.Sqrt,
                                    bias=mk.c_eps[0:1, 0:1]), "sp")
            self.on("scalar", s_fn2)
            nss = self.val("sp")

            def v_fn3(v, sl=sl, nss=nss):
                mk.w(v, "sp", nss)
                mk.inc(v.reciprocal(mk.strow[0:1, 512:512 + sl],
                                    mk.strow[0:1, 1024:1024 + sl]), "vp")
            self.on("vector", v_fn3)
            nvr = self.val("vp")

            def t_fn2(t, sp0=sp0, sl=sl, nvr=nvr):
                mk.w(t, "vp", nvr)
                t.matmul(mk.ps_a[:, sp0:sp0 + sl], mk.ones1_f[:, :],
                         mk.strow[0:1, 0:sl], start=True, stop=True)
                mk.inc(t.matmul(mk.ps_b[:, sp0:sp0 + sl], mk.ones1_f[:, :],
                                mk.strow[0:1, 512:512 + sl],
                                start=True, stop=True), "tp")
            self.on("tensor", t_fn2)
            tp_bc = self.val("tp")

        x2_free = [list(xs_free[0]), list(xs_free[1])]
        vlast = None
        for k in range(KT):
            slot = k % 2

            def sy_fn2(sy, k=k, slot=slot, xf=tuple(x2_free[slot])):
                for sem, vv in xf:
                    sy.wait_ge(mk.sem(sem), vv)
                src = bass.AP(src_dram, base + (128 * k) * stride,
                              [[stride, 128], [1, tlen]])
                mk.inc(sy.dma_start(
                    mk.xstrip[:, slot * 1024:slot * 1024 + tlen], src),
                    qsem[slot], 16)
            self.on(qeng[slot], sy_fn2)
            ndx = self.val(qsem[slot])

            def v_fn4(v, k=k, slot=slot, ndx=ndx):
                mk.w(v, qsem[slot], ndx)
                if k == 0:
                    mk.w(v, "tp", tp_bc)
                for sp0, sl in spans:
                    v.tensor_tensor(
                        mk.tmp1[:, 0:sl],
                        mk.xstrip[:, slot * 1024 + sp0:slot * 1024 + sp0 + sl],
                        mk.ps_a[:, sp0:sp0 + sl], ALU.subtract)
                    mk.inc(v.tensor_tensor(
                        dst_v[:, k, sp0:sp0 + sl], mk.tmp1[:, 0:sl],
                        mk.ps_b[:, sp0:sp0 + sl], ALU.mult), "vp")
            self.on("vector", v_fn4)
            x2_free[slot] = [("vp", self.val("vp"))]
            vlast = self.val("vp")
        self.xstrip_free = [x2_free[0][0], x2_free[1][0]]
        self.ps_ab_free = ("vp", vlast)
        return vlast

    def _proj(self, wd, wtotcols, rhs_v, rhs_ready, tlen, evac):
        mk = self
        nsp = tlen // 512
        psfree = {}
        for ei in range(wtotcols // 256):
            slot = ei % 2

            def g_fn(g, ei=ei, slot=slot):
                sem, vv = mk.wfree[slot]
                g.wait_ge(mk.sem(sem), vv)
                src = bass.AP(wd, ei * 128 * 4096, [[4096, 128], [1, 4096]])
                mk.inc(g.dma_start(
                    mk.wv[:, slot * 4096:slot * 4096 + 4096], src),
                    "d_w%d" % slot, 16)
            self.on("gpsimd", g_fn)
            nw = self.val("d_w%d" % slot)

            for ot in range(2):
                oabs = ei * 2 + ot
                for sp in range(nsp):
                    pb = ((oabs * nsp + sp) % 2) * 512

                    def t_fn(t, ei=ei, slot=slot, ot=ot, sp=sp, pb=pb, nw=nw,
                             last_slot=(ot == 1 and sp == nsp - 1)):
                        mk.w(t, "d_w%d" % slot, nw)
                        mk.w(t, "vp", rhs_ready)
                        if pb in psfree:
                            t.wait_ge(mk.sem(psfree[pb][0]), psfree[pb][1])
                        last = None
                        for k in range(KT):
                            last = t.matmul(
                                mk.ps_c[:, pb:pb + 512],
                                mk.wvv()[:, slot, k, ot * 128:(ot + 1) * 128],
                                rhs_v[:, k, sp * 512:sp * 512 + 512],
                                start=(k == 0), stop=(k == KT - 1))
                        mk.inc(last, "tp")
                        if last_slot:
                            mk.wfree[slot] = ("tp", mk.cnt["tp"])
                    self.on("tensor", t_fn)
                    psfree[pb] = evac(oabs, sp, pb, self.val("tp"))

    def _attn_tile(self, is_halo, tile):
        mk = self
        TT, TH, S = self.TT, self.TH, self.S
        tlen = TH if is_halo else TT
        ncht = tlen // C
        ntb = tlen // 128
        nsp = tlen // 512
        base = 0 if is_halo else TH + tile * TT

        vp_ln = self._ln_stream(self.x_d, base, TH + S, tlen, self.xln_v)

        if not is_halo and tile == 0:
            self.tap_bf("t_xln", self.xln_v[:, :, 0:tlen],
                        lambda d: d.ap().rearrange("p (k t) -> p k t", k=KT))

        if not is_halo:
            vp_prev = self.val("vp")

            def evac_q(oabs, sp, pb, tpv):
                def s_fn(s, oabs=oabs, sp=sp, pb=pb, tpv=tpv):
                    mk.w(s, "tp", tpv)
                    mk.w(s, "vp", vp_prev)
                    mk.inc(s.activation(
                        mk.qt_v[:, oabs, sp * 512:sp * 512 + 512],
                        mk.ps_c[:, pb:pb + 512], AF.Silu), "sp")
                mk.on("scalar", s_fn)
                return ("sp", mk.val("sp"))
            self._proj(self.wdr["wq"], D, self.xln_v, vp_ln, tlen, evac_q)

        vs_free = self.vstrip_free
        psfree_v = {0: ("sp", self.val("sp")), 512: ("sp", self.val("sp"))}
        for ei in range(8):
            slot = ei % 2

            def g_fn(g, ei=ei, slot=slot):
                sem, vv = mk.wfree[slot]
                g.wait_ge(mk.sem(sem), vv)
                src = bass.AP(self.wdr["wi"], ei * 128 * 4096,
                              [[4096, 128], [1, 4096]])
                mk.inc(g.dma_start(
                    mk.wv[:, slot * 4096:slot * 4096 + 4096], src),
                    "d_w%d" % slot, 16)
            self.on("gpsimd", g_fn)
            nw = self.val("d_w%d" % slot)

            for tb in range(ntb):
                pb = (tb % 2) * 512

                def t_fn(t, ei=ei, slot=slot, tb=tb, pb=pb, nw=nw,
                         last_slot=(tb == ntb - 1)):
                    mk.w(t, "d_w%d" % slot, nw)
                    mk.w(t, "vp", vp_ln)
                    if pb in psfree_v:
                        t.wait_ge(mk.sem(psfree_v[pb][0]), psfree_v[pb][1])
                    last = None
                    for k in range(KT):
                        last = t.matmul(
                            mk.ps_c[:, pb:pb + 256],
                            mk.xln_v[:, k, tb * 128:(tb + 1) * 128],
                            mk.wvv()[:, slot, k, :],
                            start=(k == 0), stop=(k == KT - 1))
                    mk.inc(last, "tp")
                    if last_slot:
                        mk.wfree[slot] = ("tp", mk.cnt["tp"])
                self.on("tensor", t_fn)
                tpv = self.val("tp")
                ss = (ei * ntb + tb) % 2

                def v_fn(v, pb=pb, tpv=tpv, ss=ss):
                    mk.w(v, "tp", tpv)
                    v.wait_ge(mk.sem(vs_free[ss][0]), vs_free[ss][1])
                    mk.inc(v.tensor_copy(
                        mk.xbfk[:, ss * 512:ss * 512 + 256],
                        mk.ps_c[:, pb:pb + 256]), "vp")
                self.on("vector", v_fn)
                psfree_v[pb] = ("vp", self.val("vp"))
                nvv = self.val("vp")

                def sy_fn(sy, ei=ei, tb=tb, ss=ss, nvv=nvv):
                    mk.w(sy, "vp", nvv)
                    dst = bass.AP(self.vb_d, tb * 128 * D + ei * 256,
                                  [[D, 128], [1, 256]])
                    mk.inc(sy.dma_start(
                        dst, mk.xbfk[:, ss * 512:ss * 512 + 256]),
                        "d_vb", 16)
                self.on("sync", sy_fn)
                vs_free[ss] = ("d_vb", self.val("d_vb"))
        dvb = self.val("d_vb")

        f_psfree = ("vp", self.val("vp"))
        ew_free = None
        for h in range(NH):
            ei = h // 2
            ot = h % 2
            slot = ei % 2
            if ot == 0:
                def g_fn2(g, ei=ei, slot=slot):
                    sem, vv = mk.wfree[slot]
                    g.wait_ge(mk.sem(sem), vv)
                    src = bass.AP(self.wdr["wf"], ei * 128 * 4096,
                                  [[4096, 128], [1, 4096]])
                    mk.inc(g.dma_start(
                        mk.wv[:, slot * 4096:slot * 4096 + 4096], src),
                        "d_w%d" % slot, 16)
                self.on("gpsimd", g_fn2)
            nw = self.val("d_w%d" % slot)

            def t_fn2(t, h=h, ot=ot, slot=slot, nw=nw):
                mk.w(t, "d_w%d" % slot, nw)
                mk.w(t, "vp", vp_ln)
                if f_psfree is not None:
                    t.wait_ge(mk.sem(f_psfree[0]), f_psfree[1])
                for sp0 in range(0, tlen, 512):
                    sl = min(512, tlen - sp0)
                    last = None
                    for k in range(KT):
                        last = t.matmul(
                            mk.ps_c[:, sp0:sp0 + sl],
                            mk.wvv()[:, slot, k, ot * 128:(ot + 1) * 128],
                            mk.xln_v[:, k, sp0:sp0 + sl],
                            start=(k == 0), stop=(k == KT - 1))
                    mk.inc(last, "tp")
                if ot == 1:
                    mk.wfree[slot] = ("tp", mk.cnt["tp"])
            self.on("tensor", t_fn2)
            tpf = self.val("tp")
            vp_prev2 = self.val("vp")

            def s_fn3(s, tpf=tpf, ew=ew_free):
                mk.w(s, "tp", tpf)
                if ew is not None:
                    s.wait_ge(mk.sem(ew[0]), ew[1])
                s.activation(mk.tmp1[:, 0:tlen], mk.ps_c[:, 0:tlen],
                             AF.Sigmoid, scale=-1.0)
                s.activation(mk.tmp2[:, 0:tlen], mk.ps_c[:, 0:tlen],
                             AF.Sigmoid)
                mk.inc(s.activation(mk.tmp2[:, 0:tlen], mk.tmp2[:, 0:tlen],
                                    AF.Ln), "sp")
            self.on("scalar", s_fn3)
            f_psfree = ("sp", self.val("sp"))
            spg = self.val("sp")

            def v_fn5(v, h=h, spg=spg, vp_prev2=vp_prev2):
                mk.w(v, "sp", spg)
                mk.w(v, "gp", mk.gp_consts)
                if h == 0:
                    mk.w(v, "vp", vp_prev2)
                for cc in range(ncht):
                    v.reduce_sum(
                        mk.ech[:, h * mk.NCH + cc:h * mk.NCH + cc + 1],
                        mk.tmp2[:, cc * C:(cc + 1) * C], axis=AX.X)
                for cc in range(ncht):
                    v.tensor_tensor_scan(
                        mk.tmp3[:, cc * C:(cc + 1) * C],
                        mk.tmp2[:, cc * C:(cc + 1) * C],
                        mk.zeros_c[:, :], 0.0, ALU.add, ALU.add)
                mk.inc(v.tensor_scalar(mk.tmp2[:, 0:tlen], mk.tmp3[:, 0:tlen],
                                       -1.0, CLAMP, ALU.mult, ALU.min), "vp")
            self.on("vector", v_fn5)
            nvc = self.val("vp")

            def s_fn4(s, h=h, nvc=nvc, is_halo=is_halo):
                mk.w(s, "vp", nvc)
                s.activation(mk.tmp2[:, 0:tlen], mk.tmp2[:, 0:tlen], AF.Exp)
                if not is_halo:
                    s.activation(mk.tmp3[:, 0:tlen], mk.tmp3[:, 0:tlen],
                                 AF.Exp, bias=mk.c_lnsc[:, 0:1])
                mk.inc(s.activation(
                    mk.ech[:, h * mk.NCH:h * mk.NCH + ncht],
                    mk.ech[:, h * mk.NCH:h * mk.NCH + ncht], AF.Exp), "sp")
            self.on("scalar", s_fn4)
            nse = self.val("sp")

            def v_fn6(v, h=h, nse=nse, is_halo=is_halo):
                mk.w(v, "sp", nse)
                mk.inc(v.tensor_tensor(mk.kt_v[:, h, 0:tlen],
                                       mk.tmp1[:, 0:tlen],
                                       mk.tmp2[:, 0:tlen], ALU.mult), "vp")
                if not is_halo:
                    mk.inc(v.tensor_tensor(mk.qt_v[:, h, 0:tlen],
                                           mk.qt_v[:, h, 0:tlen],
                                           mk.tmp3[:, 0:tlen], ALU.mult),
                           "vp")
            self.on("vector", v_fn6)
            ew_free = ("vp", self.val("vp"))
        vp_ew = self.val("vp")
        self.xstrip_free = [("vp", vp_ew), ("vp", vp_ew)]
        if not is_halo and tile == 0:
            self.tap_bf("t_qt", self.qt_v[:, :, 0:tlen],
                        lambda d: d.ap().rearrange("p (k t) -> p k t", k=NH))
            self.tap_bf("t_kt", self.kt_v[:, :, 0:tlen],
                        lambda d: d.ap().rearrange("p (k t) -> p k t", k=NH))
            self.tap_bf("t_ech", self.ech[:, :])

        self._scan_tap = (not is_halo) and tile == 0
        self._scan(tlen, ncht, is_halo, vp_ew, dvb)
        self._scan_tap = False

        if not is_halo and tile == 0:
            self.tap_bf("t_ofm", self.ofm_v[:, :, 0:tlen],
                        lambda d: d.ap().rearrange("p (k t) -> p k t", k=KT))
            self.tap_bf("t_hst", self.h_st[:, :])
        if is_halo:
            def v_halo(v, tpn=self.val("tp"), vpn=self.val("vp"),
                       spn=self.val("sp")):
                mk.w(v, "tp", tpn)
                mk.w(v, "vp", vpn)
                mk.w(v, "sp", spn)
                v.tensor_scalar(mk.h_st[:, :], mk.h_st[:, :],
                                mk.halo_sb[:, 0:1], None, ALU.mult)
                mk.inc(v.tensor_copy(mk.h_bf[:, :], mk.h_st[:, :]), "vp")
            self.on("vector", v_halo)
            self.tap_bf("t_hhalo", self.h_st[:, :])
        else:
            self._post(tile)

    def _scan(self, tlen, ncht, is_halo, vp_ew, dvb):
        mk = self
        v2half_loaded = -1
        psA_free = ("vp", vp_ew)
        tpO_prev = self.val("tp")
        pend_tr = None
        sp0 = self.val("sp")
        h_ready = {0: ("sp", sp0), 1: ("sp", sp0)}
        psB_free = ("sp", sp0)
        for c in range(ncht):
            half = c // 8
            cl = c % 8
            if half != v2half_loaded:
                tp_now = self.val("tp")
                nld = min(8, ncht - half * 8)

                def sy_v2(sy, half=half, tp_now=tp_now, nld=nld):
                    mk.w(sy, "d_vb", dvb)
                    sy.wait_ge(mk.sem("tp"), tp_now)
                    src = bass.AP(self.vb_d, half * 8 * C * D,
                                  [[D, C], [C * D, nld], [1, D]])
                    mk.inc(sy.dma_start(mk.v2_v[:, 0:nld, :], src),
                           "d_v2", 16)
                self.on("sync", sy_v2)
                v2half_loaded = half
            nv2 = self.val("d_v2")

            for hh in range(2):
                H = list(range(hh * 8, hh * 8 + 8))

                if not is_halo:
                    def t_A(t, c=c, H=H, paf=psA_free, nv2=nv2):
                        t.wait_ge(mk.sem(paf[0]), paf[1])
                        mk.w(t, "d_v2", nv2)
                        last = None
                        for j, h in enumerate(H):
                            last = t.matmul(
                                mk.ps_a[0:C, j * 64:(j + 1) * 64],
                                mk.kt_v[:, h, c * C:(c + 1) * C],
                                mk.qt_v[:, h, c * C:(c + 1) * C],
                                start=True, stop=True)
                        mk.inc(last, "tp")
                    self.on("tensor", t_A)
                tpA = self.val("tp")

                if not is_halo:
                    def v_mk(v, c=c, tpA=tpA):
                        mk.w(v, "tp", tpA)
                        mk.inc(v.tensor_tensor(mk.am_sb[:, :],
                                               mk.ps_a[0:C, 0:512],
                                               mk.maskr[:, 0:512], ALU.mult),
                               "vp")
                    self.on("vector", v_mk)
                nvk = self.val("vp")
                psA_free = ("vp", nvk)

                sp_gate = max(h_ready[hh][1], psB_free[1])

                def t_O(t, c=c, cl=cl, H=H, nvk=nvk, nv2=nv2,
                        sp_gate=sp_gate, vp_ew=vp_ew, is_halo=is_halo):
                    mk.w(t, "sp", sp_gate)
                    if not is_halo:
                        mk.w(t, "vp", nvk)
                    else:
                        mk.w(t, "vp", vp_ew)
                    mk.w(t, "d_v2", nv2)
                    last = None
                    for j, h in enumerate(H):
                        if not is_halo:
                            t.matmul(mk.ps_b[0:C, j * 128:(j + 1) * 128],
                                     mk.qt_v[:, h, c * C:(c + 1) * C],
                                     mk.h_bf[:, h * DK:(h + 1) * DK],
                                     start=True, stop=False)
                            t.matmul(mk.ps_b[0:C, j * 128:(j + 1) * 128],
                                     mk.am_sb[0:C, j * C:(j + 1) * C],
                                     mk.v2_v[:, cl, h * DK:(h + 1) * DK],
                                     start=False, stop=True)
                        last = t.transpose(
                            mk.ps_tr2[0:C, j * 128:(j + 1) * 128],
                            mk.kt_v[:, h, c * C:(c + 1) * C],
                            mk.ident_bf[:, :])
                    mk.inc(last, "tp")
                self.on("tensor", t_O)
                tpO = self.val("tp")
                tpO_prev = tpO

                ntr_guard = None
                if hh == 0 and pend_tr is not None:
                    ntr_guard = pend_tr()
                    pend_tr = None

                def v_ev(v, tpO=tpO):
                    mk.w(v, "tp", tpO)
                    mk.inc(v.tensor_copy(mk.ktm_sb[:, :],
                                         mk.ps_tr2[0:C, :]), "vp")
                self.on("vector", v_ev)
                nve = self.val("vp")

                if not is_halo:
                    def s_ev(s, hh=hh, tpO=tpO, ntr_guard=ntr_guard):
                        mk.w(s, "tp", tpO)
                        if ntr_guard is not None:
                            mk.w(s, "tp", ntr_guard)
                        mk.inc(s.copy(
                            mk.otmp[0:C, hh * 1024:hh * 1024 + 1024],
                            mk.ps_b[0:C, 0:1024]), "sp")
                    self.on("scalar", s_ev)
                    psB_free = ("sp", self.val("sp"))
                if (not is_halo) and c == 0 and hh == 0 and \
                        getattr(self, "_scan_tap", False):
                    self.tap_bf("t_otmp0", mk.otmp[0:C, 0:1024])
                if (not is_halo) and c == 0 and hh == 1 and \
                        getattr(self, "_scan_tap", False):
                    self.tap_bf("t_otmp1", mk.otmp[0:C, :])
                    self.tap_bf("t_am0", mk.am_sb[:, :])
                    self.tap_bf("t_ktm0", mk.ktm_sb[:, :])

                def t_S(t, c=c, cl=cl, H=H, nve=nve):
                    mk.w(t, "vp", nve)
                    last = None
                    for j, h in enumerate(H):
                        last = t.matmul(
                            mk.ps_c[:, j * 128:(j + 1) * 128],
                            mk.ktm_sb[:, j * DK:(j + 1) * DK],
                            mk.v2_v[:, cl, h * DK:(h + 1) * DK],
                            start=True, stop=True)
                    mk.inc(last, "tp")
                self.on("tensor", t_S)
                tpS = self.val("tp")

                def v_h(v, c=c, hh=hh, H=H, tpS=tpS):
                    mk.w(v, "tp", tpS)
                    v.tensor_tensor(
                        mk.h_st[:, hh * 1024:hh * 1024 + 1024],
                        mk.h_st[:, hh * 1024:hh * 1024 + 1024],
                        mk.ps_c[:, 0:1024], ALU.add)
                    last = None
                    for j, h in enumerate(H):
                        last = v.tensor_scalar(
                            mk.h_st[:, h * DK:(h + 1) * DK],
                            mk.h_st[:, h * DK:(h + 1) * DK],
                            mk.ech[:, h * mk.NCH + c:h * mk.NCH + c + 1],
                            None, ALU.mult)
                    mk.inc(last, "vp")
                self.on("vector", v_h)
                nvh2 = self.val("vp")

                def s_hb(s, hh=hh, nvh2=nvh2):
                    mk.w(s, "vp", nvh2)
                    mk.inc(s.copy(
                        mk.h_bf[:, hh * 1024:hh * 1024 + 1024],
                        mk.h_st[:, hh * 1024:hh * 1024 + 1024]), "sp")
                self.on("scalar", s_hb)
                h_ready[hh] = ("sp", self.val("sp"))

            if not is_halo:
                def s_sq1(s):
                    s.activation(mk.sqk[0:C, :], mk.otmp[:, 0:1024],
                                 AF.Square)
                    mk.inc(s.activation(mk.xbfk[0:C, :],
                                        mk.otmp[:, 1024:2048], AF.Square),
                           "sp")
                self.on("scalar", s_sq1)
                n1 = self.val("sp")

                def v_red2(v, n1=n1):
                    mk.w(v, "sp", n1)
                    v.tensor_tensor(mk.sqk[0:C, :].bitcast(F32)
                                    if False else mk.sqk[0:C, :],
                                    mk.sqk[0:C, :], mk.xbfk[0:C, :], ALU.add)
                    mk.inc(v.reduce_sum(mk.rms_sc[:, 0:1], mk.sqk[0:C, :],
                                        axis=AX.X), "vp")
                self.on("vector", v_red2)
                n4 = self.val("vp")

                def s_rms(s, n4=n4):
                    mk.w(s, "vp", n4)
                    mk.inc(s.activation(mk.rms_sc[:, 1:2], mk.rms_sc[:, 0:1],
                                        AF.Sqrt, scale=mk.c_invd[0:C, 0:1],
                                        bias=mk.c_reps[0:C, 0:1]),
                           "sp")
                self.on("scalar", s_rms)
                n5 = self.val("sp")

                def v_recip(v, n5=n5):
                    mk.w(v, "sp", n5)
                    mk.inc(v.reciprocal(mk.rms_sc[:, 0:1], mk.rms_sc[:, 1:2]),
                           "vp")
                self.on("vector", v_recip)
                n6 = self.val("vp")

                def s_scale(s, n6=n6):
                    mk.w(s, "vp", n6)
                    mk.inc(s.activation(mk.otmp[:, :], mk.otmp[:, :], AF.Copy,
                                        scale=mk.rms_sc[:, 0:1]), "sp")
                self.on("scalar", s_scale)
                n7 = self.val("sp")

                def v_gw(v, n7=n7):
                    mk.w(v, "sp", n7)
                    mk.inc(v.tensor_tensor(mk.otmp[:, :], mk.otmp[:, :],
                                           mk.gw_rep[0:C, :], ALU.mult), "vp")
                self.on("vector", v_gw)
                nrm = self.val("vp")

                def emit_tr(c=c, nrm=nrm):
                    def t_tr(t, c=c, nrm=nrm):
                        mk.w(t, "vp", nrm)
                        last = None
                        for vt in range(KT):
                            last = t.transpose(
                                mk.ps_tr[:, vt * 64:(vt + 1) * 64],
                                mk.otmp[0:C, vt * 128:(vt + 1) * 128],
                                mk.ident_bf[0:C, 0:C])
                        mk.inc(last, "tp")
                    self.on("tensor", t_tr)
                    ntr = self.val("tp")

                    def v_tr(v, c=c, ntr=ntr):
                        mk.w(v, "tp", ntr)
                        for vt in range(KT):
                            v.tensor_copy(mk.ofm_v[:, vt, c * C:(c + 1) * C],
                                          mk.ps_tr[:, vt * 64:(vt + 1) * 64])
                        if c == 0 and getattr(mk, "_scan_tap", False):
                            v.tensor_copy(mk.dbg1[:, :], mk.ps_tr[:, 0:64])
                            v.tensor_copy(mk.dbg2[:, :], mk.otmp[0:C, 0:128])
                        mk.inc(v.engine_nop(), "vp")
                    self.on("vector", v_tr)
                    if c == 0 and getattr(self, "_scan_tap", False):
                        self.tap_bf("t_dbg1", mk.dbg1[:, :])
                        self.tap_bf("t_dbg2", mk.dbg2[:, :])
                        self.tap_bf("t_otmpr", mk.otmp[0:C, :])
                        self.tap_bf("t_ofm0", mk.ofm_v[:, 0:2, 0:C],
                                    lambda dd: dd.ap().rearrange(
                                        "p (k t) -> p k t", k=2))
                    return ntr
                if c == ncht - 1:
                    emit_tr()
                else:
                    pend_tr = emit_tr

    def _post(self, tile):
        mk = self
        TT, TH, S = self.TT, self.TH, self.S
        nsp = TT // 512
        base = TH + tile * TT
        vp_ofm = self.val("vp")
        self._ln2_first = True

        def evac_wo(oabs, sp, pb, tpv):
            sslot = (oabs * nsp + sp) % 2

            def s_fn(s, oabs=oabs, sp=sp, pb=pb, tpv=tpv, sslot=sslot):
                mk.w(s, "tp", tpv)
                s.activation(mk.ao_v[:, oabs, sp * 512:sp * 512 + 512],
                             mk.ps_c[:, pb:pb + 512], AF.Copy)
                mk.inc(s.activation(
                    mk.sqk[:, sslot * 512:sslot * 512 + 512],
                    mk.ps_c[:, pb:pb + 512], AF.Square), "sp")
            mk.on("scalar", s_fn)
            nsq = mk.val("sp")

            def t_fn(t, oabs=oabs, sp=sp, nsq=nsq, sslot=sslot):
                mk.w(t, "sp", nsq)
                if mk._ln2_first:
                    mk.w(t, *mk.ps_ab_free)
                    mk._ln2_first = False
                first = (oabs == 0)
                lastk = (oabs == KT - 1)
                t.matmul(mk.ps_a[0:1, sp * 512:sp * 512 + 512],
                         mk.ones_bf[:, :],
                         mk.ao_v[:, oabs, sp * 512:sp * 512 + 512],
                         start=first, stop=lastk)
                mk.inc(t.matmul(
                    mk.ps_b[0:1, sp * 512:sp * 512 + 512], mk.ones_bf[:, :],
                    mk.sqk[:, sslot * 512:sslot * 512 + 512],
                    start=first, stop=lastk), "tp")
            mk.on("tensor", t_fn)
            return ("tp", mk.val("tp"))

        self._proj(self.wdr["wo"], D, self.ofm_v, vp_ofm, TT, evac_wo)
        tp_st2 = self.val("tp")

        tp_bc2 = None
        for spq in range(nsp):
            sp0 = spq * 512

            def v_st2(v, sp0=sp0, tp_prev=tp_bc2):
                mk.w(v, "tp", tp_st2)
                if tp_prev is not None:
                    v.wait_ge(mk.sem("tp"), tp_prev)
                v.tensor_scalar(mk.strow[0:1, 0:512],
                                mk.ps_a[0:1, sp0:sp0 + 512],
                                1.0 / D, None, ALU.mult)
                v.tensor_scalar(mk.strow[0:1, 512:1024],
                                mk.ps_b[0:1, sp0:sp0 + 512], 1.0 / D, None,
                                ALU.mult)
                v.tensor_tensor(mk.strow[0:1, 1024:1536],
                                mk.strow[0:1, 0:512], mk.strow[0:1, 0:512],
                                ALU.mult)
                mk.inc(v.tensor_tensor(mk.strow[0:1, 512:1024],
                                       mk.strow[0:1, 512:1024],
                                       mk.strow[0:1, 1024:1536],
                                       ALU.subtract), "vp")
            self.on("vector", v_st2)
            nv = self.val("vp")

            def s_st2(s, nv=nv):
                mk.w(s, "vp", nv)
                mk.inc(s.activation(mk.strow[0:1, 1024:1536],
                                    mk.strow[0:1, 512:1024], AF.Sqrt,
                                    bias=mk.c_eps[0:1, 0:1]), "sp")
            self.on("scalar", s_st2)
            ns = self.val("sp")

            def v_st3(v, ns=ns):
                mk.w(v, "sp", ns)
                mk.inc(v.reciprocal(mk.strow[0:1, 512:1024],
                                    mk.strow[0:1, 1024:1536]), "vp")
            self.on("vector", v_st3)
            nvr = self.val("vp")

            def t_bc2(t, sp0=sp0, nvr=nvr):
                mk.w(t, "vp", nvr)
                t.matmul(mk.ps_a[:, sp0:sp0 + 512], mk.ones1_f[:, :],
                         mk.strow[0:1, 0:512], start=True, stop=True)
                mk.inc(t.matmul(mk.ps_b[:, sp0:sp0 + 512], mk.ones1_f[:, :],
                                mk.strow[0:1, 512:1024],
                                start=True, stop=True), "tp")
            self.on("tensor", t_bc2)
            tp_bc2 = self.val("tp")

        x2_free = [self.xstrip_free[0], self.xstrip_free[1]]
        vlast = None
        for k in range(KT):
            slot = k % 2

            def sy_fn(sy, k=k, slot=slot):
                sem, vv = x2_free[slot]
                sy.wait_ge(mk.sem(sem), vv)
                src = bass.AP(self.x_d, base + (128 * k) * (TH + S),
                              [[TH + S, 128], [1, TT]])
                mk.inc(sy.dma_start(
                    mk.xstrip[:, slot * 1024:slot * 1024 + TT], src),
                    ("d_x0", "d_xg")[slot], 16)
            self.on(("sync", "gpsimd")[slot], sy_fn)
            ndx = self.val(("d_x0", "d_xg")[slot])

            def v_ap(v, k=k, slot=slot, ndx=ndx):
                mk.w(v, ("d_x0", "d_xg")[slot], ndx)
                if k == 0:
                    mk.w(v, "tp", tp_bc2)
                sem, vv = mk.ostrip_free[0]
                v.wait_ge(mk.sem(sem), vv)
                for sp in range(nsp):
                    v.tensor_tensor(mk.tmp1[:, 0:512],
                                    mk.ao_v[:, k, sp * 512:sp * 512 + 512],
                                    mk.ps_a[:, sp * 512:sp * 512 + 512],
                                    ALU.subtract)
                    v.tensor_tensor(mk.tmp1[:, 0:512], mk.tmp1[:, 0:512],
                                    mk.ps_b[:, sp * 512:sp * 512 + 512],
                                    ALU.mult)
                    mk.inc(v.tensor_tensor(
                        mk.ostrip[:, sp * 512:sp * 512 + 512],
                        mk.tmp1[:, 0:512],
                        mk.xstrip[:, slot * 1024 + sp * 512:
                                  slot * 1024 + sp * 512 + 512],
                        ALU.add), "vp")
            self.on("vector", v_ap)
            nvv = self.val("vp")
            vlast = nvv

            def sy_x1(sy, k=k, slot=slot, nvv=nvv, tile=tile):
                mk.w(sy, "vp", nvv)
                dst = bass.AP(self.x1_d, tile * TT + (128 * k) * S,
                              [[S, 128], [1, TT]])
                mk.inc(sy.dma_start(dst, mk.ostrip[:, 0:TT]), "d_x1", 16)
            self.on("sync", sy_x1)
            x2_free[slot] = ("d_x1", self.val("d_x1"))
            mk.ostrip_free[0] = ("d_x1", self.val("d_x1"))
        self.xstrip_free = [x2_free[0], x2_free[1]]
        self.ps_ab_free = ("vp", vlast)

    def _ffn_tile(self, tile):
        mk = self
        TT, S = self.TT, self.S
        nsp = TT // 512
        ntb = TT // 128
        base = tile * TT

        if tile == 0:
            nall = self.val("d_x1")

            def sy_gate(sy):
                mk.w(sy, "d_x1", nall)
            self.on("sync", sy_gate)

        vp_y = self._ln_stream(self.x1_d, base, S, TT, self.y_v)

        G = 4
        rt_psfree = {}
        vp_prev = self.val("vp")
        sp_now = self.val("sp")
        tp_now = self.val("tp")
        vp_now = self.val("vp")
        chain_free = {j: ("sp", sp_now) for j in range(G)}
        cbf_free = {j: ("tp", tp_now) for j in range(G)}
        ctps_free = {j: ("vp", vp_now) for j in range(G)}
        sc = mk.sc_sb
        vsteps = [
            lambda v, B: v.tensor_tensor(sc[:, B + 8:B + 16],
                                         sc[:, B + 0:B + 8],
                                         mk.ebias_sb[:, :], ALU.add),
            lambda v, B: v.reduce_max(sc[:, B + 24:B + 25],
                                      sc[:, B + 16:B + 24], axis=AX.X),
            lambda v, B: v.tensor_scalar(sc[:, B + 26:B + 34],
                                         sc[:, B + 16:B + 24],
                                         sc[:, B + 25:B + 26], None,
                                         ALU.is_equal),
            lambda v, B: v.scalar_tensor_tensor(sc[:, B + 42:B + 50],
                                                sc[:, B + 34:B + 42], -1e30,
                                                sc[:, B + 16:B + 24],
                                                ALU.mult, ALU.add),
            lambda v, B: v.reduce_max(sc[:, B + 58:B + 59],
                                      sc[:, B + 50:B + 58], axis=AX.X),
            lambda v, B: v.tensor_scalar(sc[:, B + 60:B + 68],
                                         sc[:, B + 16:B + 24],
                                         sc[:, B + 59:B + 60], None,
                                         ALU.is_ge),
            lambda v, B: v.tensor_tensor(sc[:, B + 76:B + 84],
                                         sc[:, B + 0:B + 8],
                                         sc[:, B + 68:B + 76], ALU.mult),
            lambda v, B: v.reduce_sum(sc[:, B + 84:B + 85],
                                      sc[:, B + 85:B + 93], axis=AX.X),
            lambda v, B: v.tensor_scalar(sc[:, B + 94:B + 95],
                                         sc[:, B + 93:B + 94], 1e-9, None,
                                         ALU.add),
            lambda v, B: v.reciprocal(sc[:, B + 96:B + 97],
                                      sc[:, B + 95:B + 96]),
            lambda v, B: v.tensor_scalar(sc[:, B + 98:B + 106],
                                         sc[:, B + 85:B + 93],
                                         sc[:, B + 97:B + 98], None,
                                         ALU.mult),
        ]
        scopies = [[(16, 8, 8)], [(25, 24, 1)], [(34, 26, 8)],
                   [(50, 42, 8)], [(59, 58, 1)], [(68, 60, 8)],
                   [(85, 76, 8)], [(93, 84, 1)], [(95, 94, 1)],
                   [(97, 96, 1)], None]
        last_ct = None
        for g0 in range(0, ntb, G):
            tbs = list(range(g0, min(g0 + G, ntb)))
            pv = {}
            psv = {}
            for tb in tbs:
                pb = (tb % 2) * 512

                def t_rt(t, tb=tb, pb=pb):
                    mk.w(t, "vp", vp_y)
                    mk.w(t, "d_cst", mk.d_cst_all)
                    if pb in rt_psfree:
                        t.wait_ge(mk.sem(rt_psfree[pb][0]), rt_psfree[pb][1])
                    last = None
                    for k in range(KT):
                        last = t.matmul(
                            mk.ps_c[:, pb:pb + 8],
                            mk.y_v[:, k, tb * 128:(tb + 1) * 128],
                            mk.rwt_sb.ap().rearrange("p (k e) -> p k e",
                                                     k=KT)[:, k, :],
                            start=(k == 0), stop=(k == KT - 1))
                    mk.inc(last, "tp")
                self.on("tensor", t_rt)
                ntp = self.val("tp")

                def s_rt(s, tb=tb, pb=pb, ntp=ntp, cf=chain_free[tb % G]):
                    mk.w(s, "tp", ntp)
                    s.wait_ge(mk.sem(cf[0]), cf[1])
                    mk.inc(s.activation(
                        mk.sc_sb[:, (tb % G) * 128:(tb % G) * 128 + 8],
                        mk.ps_c[:, pb:pb + 8], AF.Sigmoid), "sp")
                self.on("scalar", s_rt)
                psv[tb] = self.val("sp")
                rt_psfree[pb] = ("sp", psv[tb])

            for step in range(len(vsteps)):
                for tb in tbs:
                    def v_f(v, tb=tb, step=step, w=psv[tb]):
                        mk.w(v, "sp", w)
                        mk.inc(vsteps[step](v, (tb % G) * 128), "vp")
                    mk.on("vector", v_f)
                    pv[tb] = self.val("vp")
                if scopies[step] is None:
                    continue
                for tb in tbs:
                    def s_f(s, tb=tb, step=step, w=pv[tb]):
                        mk.w(s, "vp", w)
                        B = (tb % G) * 128
                        last = None
                        for dst, srcc, n in scopies[step]:
                            last = s.copy(sc[:, B + dst:B + dst + n],
                                          sc[:, B + srcc:B + srcc + n])
                        mk.inc(last, "sp")
                    mk.on("scalar", s_f)
                    psv[tb] = self.val("sp")

            for tb in tbs:
                j = tb % G

                def s_cbf(s, tb=tb, j=j, w=pv[tb], pct=cbf_free[j]):
                    mk.w(s, "vp", w)
                    s.wait_ge(mk.sem(pct[0]), pct[1])
                    mk.inc(s.copy(mk.cbf[:, j * 8:j * 8 + 8],
                                  sc[:, (tb % G) * 128 + 98:
                                     (tb % G) * 128 + 106]), "sp")
                self.on("scalar", s_cbf)
                nvb = self.val("sp")
                chain_free[j] = ("sp", nvb)

                def t_ct(t, j=j, nvb=nvb, cpf=ctps_free[j]):
                    mk.w(t, "sp", nvb)
                    t.wait_ge(mk.sem(cpf[0]), cpf[1])
                    mk.inc(t.transpose(
                        mk.ps_tr[0:8, j * 128:j * 128 + 128],
                        mk.cbf[:, j * 8:j * 8 + 8],
                        mk.ident_bf[:, :]), "tp")
                self.on("tensor", t_ct)
                ntt = self.val("tp")
                cbf_free[j] = ("tp", ntt)

                def v_ct(v, tb=tb, j=j, ntt=ntt):
                    mk.w(v, "tp", ntt)
                    mk.inc(v.tensor_copy(
                        mk.cT_sb[:, tb * 128:(tb + 1) * 128],
                        mk.ps_tr[0:8, j * 128:j * 128 + 128]), "vp")
                self.on("vector", v_ct)
                last_ct = ("vp", self.val("vp"))
                ctps_free[j] = last_ct
            if tbs[-1] == ntb - 1 and tile == 0 and "t_sc" in self.tap_d:
                self.tap_bf("t_sc", self.sc_sb[:, 0:128])
        nct = self.val("vp")

        cb_psfree = {}
        crow_free = [("tp", 0), ("tp", 0)]
        for e in range(E):
            cs = e % 2

            def sy_cr(sy, e=e, cs=cs):
                if e < 2:
                    mk.w(sy, "vp", nct)
                sem, vv = crow_free[cs]
                sy.wait_ge(mk.sem(sem), vv)
                mk.inc(sy.dma_start(
                    mk.crow[0:1, cs * 1024:cs * 1024 + TT],
                    mk.cT_sb[e:e + 1, 0:TT]), "d_cr", 16)
            self.on("sync", sy_cr)
            ncr = self.val("d_cr")

            for sp in range(nsp):
                pb = ((e * nsp + sp) % 2) * 512

                def t_cb(t, e=e, cs=cs, sp=sp, pb=pb, ncr=ncr,
                         vp_prev=vp_prev):
                    mk.w(t, "d_cr", ncr)
                    mk.w(t, "vp", vp_prev)
                    if pb in cb_psfree:
                        t.wait_ge(mk.sem(cb_psfree[pb][0]), cb_psfree[pb][1])
                    mk.inc(t.matmul(
                        mk.ps_c[:, pb:pb + 512], mk.ones1_bf[:, :],
                        mk.crow[0:1, cs * 1024 + sp * 512:
                                cs * 1024 + sp * 512 + 512],
                        start=True, stop=True), "tp")
                self.on("tensor", t_cb)
                if sp == nsp - 1:
                    crow_free[cs] = ("tp", self.val("tp"))
                ntc = self.val("tp")

                def v_cb(v, e=e, sp=sp, pb=pb, ntc=ntc):
                    mk.w(v, "tp", ntc)
                    mk.inc(v.tensor_copy(
                        mk.cbc_v[:, e, sp * 512:sp * 512 + 512],
                        mk.ps_c[:, pb:pb + 512]), "vp")
                self.on("vector", v_cb)
                cb_psfree[pb] = ("vp", self.val("vp"))
        vp_cbc = self.val("vp")
        if tile == 0 and "t_cbc" in self.tap_d:
            self.tap_bf("t_cbc", self.cbc_v[:, :, 0:TT],
                        lambda dd: dd.ap().rearrange("p (k t) -> p k t", k=E))

        gu_free = {0: ("vp", vp_cbc), 1: mk.ps_ab_free}
        sq_free = {0: ("tp", 0), 1: ("tp", 0)}
        pstrip_free = self.vstrip_free
        for ht in range(HST):
            slot = ht % 2

            def g_w(g, ht=ht, slot=slot):
                sem, vv = mk.wfree[slot]
                g.wait_ge(mk.sem(sem), vv)
                src = bass.AP(self.sw12_d, ht * 128 * 4096,
                              [[4096, 128], [1, 4096]])
                mk.inc(g.dma_start(
                    mk.wv[:, slot * 4096:slot * 4096 + 4096], src),
                    "d_w%d" % slot, 16)
            self.on("gpsimd", g_w)
            nw = self.val("d_w%d" % slot)

            for sp in range(nsp):
                pset = (ht * nsp + sp) % 2
                if pset == 0:
                    gp_ap = mk.ps_c[:, 0:512]
                    up_ap = mk.ps_c[:, 512:1024]
                else:
                    gp_ap = mk.ps_a[:, 0:512]
                    up_ap = mk.ps_b[:, 0:512]

                def t_gu(t, ht=ht, slot=slot, sp=sp, nw=nw,
                         gp_ap=gp_ap, up_ap=up_ap, gu=gu_free[pset],
                         last_slot=(sp == nsp - 1)):
                    mk.w(t, "d_w%d" % slot, nw)
                    mk.w(t, "vp", vp_y)
                    t.wait_ge(mk.sem(gu[0]), gu[1])
                    last = None
                    for k in range(KT):
                        last = t.matmul(
                            gp_ap, mk.wvv()[:, slot, k, 0:128],
                            mk.y_v[:, k, sp * 512:sp * 512 + 512],
                            start=(k == 0), stop=(k == KT - 1))
                    for k in range(KT):
                        last = t.matmul(
                            up_ap, mk.wvv()[:, slot, k, 128:256],
                            mk.y_v[:, k, sp * 512:sp * 512 + 512],
                            start=(k == 0), stop=(k == KT - 1))
                    mk.inc(last, "tp")
                    if last_slot:
                        mk.wfree[slot] = ("tp", mk.cnt["tp"])
                self.on("tensor", t_gu)
                ntg = self.val("tp")

                def s_gu(s, ntg=ntg, gp_ap=gp_ap, pset=pset,
                         sqf=sq_free[pset]):
                    mk.w(s, "tp", ntg)
                    s.wait_ge(mk.sem(sqf[0]), sqf[1])
                    mk.inc(s.activation(
                        mk.sqk[:, pset * 512:pset * 512 + 512], gp_ap,
                        AF.Silu), "sp")
                self.on("scalar", s_gu)
                nsg = self.val("sp")
                ss = (ht * nsp + sp) % 2

                def v_gu(v, nsg=nsg, ss=ss, up_ap=up_ap, pset=pset):
                    mk.w(v, "sp", nsg)
                    sem, vv = pstrip_free[ss]
                    v.wait_ge(mk.sem(sem), vv)
                    mk.inc(v.tensor_tensor(
                        mk.xbfk[:, ss * 512:ss * 512 + 512],
                        mk.sqk[:, pset * 512:pset * 512 + 512], up_ap,
                        ALU.mult), "vp")
                self.on("vector", v_gu)
                gu_free[pset] = ("vp", self.val("vp"))
                sq_free[pset] = ("vp", self.val("vp"))
                nvp = self.val("vp")

                def sy_p(sy, ht=ht, sp=sp, ss=ss, nvp=nvp):
                    mk.w(sy, "vp", nvp)
                    dst = bass.AP(self.p_d, ht * 128 * S + base + sp * 512,
                                  [[S, 128], [1, 512]])
                    mk.inc(sy.dma_start(
                        dst, mk.xbfk[:, ss * 512:ss * 512 + 512]),
                        "d_p", 16)
                self.on("sync", sy_p)
                pstrip_free[ss] = ("d_p", self.val("d_p"))
                self.vstrip_free[ss] = pstrip_free[ss]
        self.ps_ab_free = ("vp", self.val("vp"))

        h12_free = {}
        w3_free = {}
        h12_free[("h12", 0)] = ("vp", self.val("vp"))
        h12_free[("h12", 512)] = ("vp", self.val("vp"))
        for eq in range(4):
            self._b5_quarter(tile, eq, vp_y, vp_cbc, h12_free, w3_free)
        self.ps_ab_free = ("vp", self.val("vp"))

    def _b5_quarter(self, tile, eq, vp_y, vp_cbc, h12_free, w3_free):
        mk = self
        TT, S = self.TT, self.S
        nsp = TT // 512
        base = tile * TT
        for e in (2 * eq, 2 * eq + 1):
            for ei in range(8):
                slot = ei % 2

                def g_w12(g, e=e, ei=ei, slot=slot):
                    sem, vv = mk.wfree[slot]
                    g.wait_ge(mk.sem(sem), vv)
                    src = bass.AP(self.w12t_d, (e * 8 + ei) * 128 * 4096,
                                  [[4096, 128], [1, 4096]])
                    mk.inc(g.dma_start(
                        mk.wv[:, slot * 4096:slot * 4096 + 4096], src),
                        "d_w%d" % slot, 16)
                self.on("gpsimd", g_w12)
                nw = self.val("d_w%d" % slot)

                for ot in range(2):
                    habs = ei * 2 + ot
                    for sp in range(nsp):
                        pb = ((habs * nsp + sp) % 2) * 512

                        def t_h12(t, e=e, ei=ei, slot=slot, ot=ot, sp=sp,
                                  pb=pb, nw=nw,
                                  last_slot=(ot == 1 and sp == nsp - 1)):
                            mk.w(t, "d_w%d" % slot, nw)
                            mk.w(t, "vp", vp_y)
                            key = ("h12", pb)
                            if key in h12_free:
                                t.wait_ge(mk.sem(h12_free[key][0]),
                                          h12_free[key][1])
                            last = None
                            for k in range(KT):
                                last = t.matmul(
                                    mk.ps_c[:, pb:pb + 512],
                                    mk.wvv()[:, slot, k,
                                             ot * 128:(ot + 1) * 128],
                                    mk.y_v[:, k, sp * 512:sp * 512 + 512],
                                    start=(k == 0), stop=(k == KT - 1))
                            mk.inc(last, "tp")
                            if last_slot:
                                mk.wfree[slot] = ("tp", mk.cnt["tp"])
                        self.on("tensor", t_h12)
                        nth = self.val("tp")

                        if habs < 8:
                            def s_h1(s, e=e, habs=habs, sp=sp, pb=pb,
                                     nth=nth):
                                mk.w(s, "tp", nth)
                                mk.inc(s.activation(
                                    mk.s1_v[:, (e % 2) * 8 + habs,
                                            sp * 512:sp * 512 + 512],
                                    mk.ps_c[:, pb:pb + 512], AF.Silu), "sp")
                            self.on("scalar", s_h1)
                            h12_free[("h12", pb)] = ("sp", self.val("sp"))
                        else:
                            hti = habs - 8

                            def v_h2(v, e=e, hti=hti, sp=sp, pb=pb, nth=nth):
                                mk.w(v, "tp", nth)
                                mk.w(v, "vp", vp_cbc)
                                v.tensor_tensor(
                                    mk.tmp1[:, 0:512],
                                    mk.s1_v[:, (e % 2) * 8 + hti,
                                            sp * 512:sp * 512 + 512],
                                    mk.ps_c[:, pb:pb + 512], ALU.mult)
                                mk.inc(v.tensor_tensor(
                                    mk.hq_v[:, (e % 2) * 8 + hti,
                                            sp * 512:sp * 512 + 512],
                                    mk.tmp1[:, 0:512],
                                    mk.cbc_v[:, e, sp * 512:sp * 512 + 512],
                                    ALU.mult), "vp")
                            self.on("vector", v_h2)
                            h12_free[("h12", pb)] = ("vp", self.val("vp"))
        vp_hq = self.val("vp")

        for de in range(8):
            nws = []
            for ie, e in enumerate((2 * eq, 2 * eq + 1)):
                slot = ie

                def g_w3(g, e=e, de=de, slot=slot):
                    sem, vv = mk.wfree[slot]
                    g.wait_ge(mk.sem(sem), vv)
                    src = bass.AP(self.w3t_d, (e * 8 + de) * 128 * 2048,
                                  [[2048, 128], [1, 2048]])
                    mk.inc(g.dma_start(
                        mk.wv[:, slot * 4096:slot * 4096 + 2048], src),
                        "d_w%d" % slot, 16)
                self.on("gpsimd", g_w3)
                nws.append(self.val("d_w%d" % slot))

            for sp in range(nsp):
                pst = mk.ps_a if (de % 2) == 0 else mk.ps_b
                key = ("w3", de % 2)

                def t_w3(t, eq=eq, de=de, sp=sp, pst=pst, nws=tuple(nws),
                         key=key, vp_hq=vp_hq):
                    mk.w(t, "d_w0", nws[0])
                    mk.w(t, "d_w1", nws[1])
                    mk.w(t, "vp", vp_hq)
                    if key in w3_free:
                        t.wait_ge(mk.sem(w3_free[key][0]), w3_free[key][1])
                    last = None
                    for dt in range(2):
                        for ie in range(2):
                            for ht in range(8):
                                last = t.matmul(
                                    pst[:, dt * 512:dt * 512 + 512],
                                    mk.wvv()[:, ie, ht,
                                             dt * 128:dt * 128 + 128],
                                    mk.hq_v[:, ie * 8 + ht,
                                            sp * 512:sp * 512 + 512],
                                    start=(ie == 0 and ht == 0),
                                    stop=(ie == 1 and ht == 7))
                        mk.inc(last, "tp")
                    if sp == nsp - 1:
                        mk.wfree[0] = ("tp", mk.cnt["tp"])
                        mk.wfree[1] = ("tp", mk.cnt["tp"])
                self.on("tensor", t_w3)
                ntw = self.val("tp")

                def v_w3(v, pst=pst, ntw=ntw, of0=self.ostrip_free[0]):
                    mk.w(v, "tp", ntw)
                    v.wait_ge(mk.sem(of0[0]), of0[1])
                    v.tensor_copy(mk.ostrip[:, 0:512], pst[:, 0:512])
                    mk.inc(v.tensor_copy(mk.ostrip[:, 512:1024],
                                         pst[:, 512:1024]), "vp")
                self.on("vector", v_w3)
                w3_free[key] = ("vp", self.val("vp"))
                nvw = self.val("vp")

                def g_fa(g, eq=eq, de=de, sp=sp, nvw=nvw, base=base):
                    mk.w(g, "vp", nvw)
                    dst = bass.AP(self.ffn_d, de * 256 * S + base + sp * 512,
                                  [[S, 128], [128 * S, 2], [1, 512]])
                    src = mk.ostrip.ap().rearrange(
                        "p (i t) -> p i t", i=2)[:, :, 0:512]
                    mk.inc(g.dma_start(
                        dst, src,
                        accum_op=(ALU.bypass if eq == 0 else ALU.add)),
                        "d_fa", 16)
                self.on("gpsimd", g_fa)
                self.ostrip_free[0] = ("d_fa", self.val("d_fa"))

    def _b4(self):
        mk = self
        S = self.S
        nts = S // 256
        tp_b5 = self.val("tp")
        vp_b5 = self.val("vp")
        ndp = self.val("d_p")

        def sy_gate(sy):
            mk.w(sy, "d_p", ndp)
        self.on("sync", sy_gate)
        p_free = [("tp", tp_b5), ("tp", tp_b5)]
        b4_ps = None
        for dq in range(4):
            tp_now = self.val("tp")

            def g_sw3(g, dq=dq, tp_now=tp_now):
                g.wait_ge(mk.sem("tp"), tp_now)
                if dq == 0 and "d_tap" in mk.sems:
                    mk.w(g, "d_tap")
                src_a = bass.AP(self.sw3t_d, (dq * 2 + 0) * 128 * 16384,
                                [[16384, 128], [1, 16384]])
                src_b = bass.AP(self.sw3t_d, (dq * 2 + 1) * 128 * 16384,
                                [[16384, 128], [1, 16384]])
                mk.inc(g.dma_start(
                    self.P1.ap().bitcast(BF16)[:, :], src_a), "d_s3", 16)
                mk.inc(g.dma_start(
                    self.P2.ap().bitcast(BF16)[:, :], src_b), "d_s3", 16)
            self.on("gpsimd", g_sw3)
            sw_need = self.val("d_s3")

            for ts in range(nts):
                slot = ts % 2

                def sy_pl(sy, ts=ts, slot=slot):
                    sem, vv = p_free[slot]
                    sy.wait_ge(mk.sem(sem), vv)
                    src = bass.AP(self.p_d, ts * 256,
                                  [[S, 128], [128 * S, HST], [1, 256]])
                    mk.inc(sy.dma_start(mk.pb4_v[slot][:, :, 0:256], src),
                           "d_pb%d" % slot, 16)
                self.on("sync", sy_pl)
                npb = self.val("d_pb%d" % slot)

                def t_b4(t, dq=dq, ts=ts, slot=slot, npb=npb,
                         sw_need=sw_need, b4_ps=b4_ps, vp_b5=vp_b5):
                    mk.w(t, "d_pb%d" % slot, npb)
                    mk.w(t, "d_s3", sw_need)
                    if dq == 0 and ts == 0:
                        mk.w(t, "vp", vp_b5)
                    if b4_ps is not None:
                        t.wait_ge(mk.sem(b4_ps[0]), b4_ps[1])
                    for dt in range(4):
                        pst = (mk.ps_a if dt < 2 else mk.ps_b)
                        pb = (dt % 2) * 512
                        last = None
                        for ht in range(HST):
                            lhs = (mk.sw3a_v if ht < 32 else mk.sw3b_v)[
                                :, ht % 32, dt * 128:(dt + 1) * 128]
                            last = t.matmul(
                                pst[:, pb:pb + 256], lhs,
                                mk.pb4_v[slot][:, ht, 0:256],
                                start=(ht == 0), stop=(ht == HST - 1))
                        mk.inc(last, "tp")
                    p_free[slot] = ("tp", mk.cnt["tp"])
                self.on("tensor", t_b4)
                ntb4 = self.val("tp")

                def v_b4(v, ntb4=ntb4, of0=self.ostrip_free[0]):
                    mk.w(v, "tp", ntb4)
                    v.wait_ge(mk.sem(of0[0]), of0[1])
                    for dt in range(4):
                        pst = (mk.ps_a if dt < 2 else mk.ps_b)
                        pb = (dt % 2) * 512
                        v.tensor_copy(mk.ostrip[:, dt * 256:dt * 256 + 256],
                                      pst[:, pb:pb + 256])
                    mk.inc(v.engine_nop(), "vp")
                self.on("vector", v_b4)
                b4_ps = ("vp", self.val("vp"))
                nvb = self.val("vp")

                def g_fa4(g, dq=dq, ts=ts, nvb=nvb):
                    mk.w(g, "vp", nvb)
                    dst = bass.AP(self.ffn_d, dq * 512 * S + ts * 256,
                                  [[S, 128], [128 * S, 4], [1, 256]])
                    src = mk.ostrip.ap().rearrange(
                        "p (i t) -> p i t", i=4)[:, :, 0:256]
                    mk.inc(g.dma_start(dst, src, accum_op=ALU.add),
                           "d_fa", 16)
                self.on("gpsimd", g_fa4)
                self.ostrip_free[0] = ("d_fa", self.val("d_fa"))
        self.ps_ab_free = ("vp", self.val("vp"))

    def _b6(self):
        mk = self
        S, TT = self.S, self.TT
        dfa = self.val("d_fa")

        def sy_gate(sy):
            mk.w(sy, "d_fa", dfa)
        self.on("sync", sy_gate)
        if "t_x1" in self.tap_d:
            def g_tx(g, dfa=dfa, dx1=self.val("d_x1")):
                mk.w(g, "d_x1", dx1)
                mk.inc(g.dma_start(self.tap_d["t_x1"].ap(), self.x1_d.ap()),
                       "d_tap", 16)
            self.on("gpsimd", g_tx)
        if "t_ffn" in self.tap_d:
            def g_tf(g, dfa=dfa):
                mk.w(g, "d_fa", dfa)
                mk.inc(g.dma_start(self.tap_d["t_ffn"].ap(), self.ffn_d.ap()),
                       "d_tap", 16)
            self.on("gpsimd", g_tf)
        for tile in range(S // TT):
            base = tile * TT
            nsp = TT // 512

            vp_ln = self._ln_stream(self.ffn_d, base, S, TT, self.xln_v)
            x2_free = [self.xstrip_free[0], self.xstrip_free[1]]
            for k in range(KT):
                slot = k % 2

                def sy_f(sy, k=k, slot=slot, base=base):
                    sem, vv = x2_free[slot]
                    sy.wait_ge(mk.sem(sem), vv)
                    src = bass.AP(self.ffn_d, base + (128 * k) * S,
                                  [[S, 128], [1, TT]])
                    mk.inc(sy.dma_start(
                        mk.xstrip[:, slot * 1024:slot * 1024 + TT], src),
                        ("d_x0", "d_xg")[slot], 16)
                self.on(("sync", "gpsimd")[slot], sy_f)
                ndf = self.val(("d_x0", "d_xg")[slot])

                def sy_x1(sy, k=k, slot=slot, base=base,
                          osf=self.ostrip_free[0]):
                    sy.wait_ge(mk.sem(osf[0]), osf[1])
                    src = bass.AP(self.x1_d, base + (128 * k) * S,
                                  [[S, 128], [1, TT]])
                    mk.inc(sy.dma_start(mk.ostrip[:, 0:TT], src),
                           "d_r%d" % slot, 16)
                self.on(("sync", "gpsimd")[slot], sy_x1)
                ndr = self.val("d_r%d" % slot)

                def v_ap(v, k=k, slot=slot, ndf=ndf, ndr=ndr):
                    mk.w(v, ("d_x0", "d_xg")[slot], ndf)
                    mk.w(v, "d_r%d" % slot, ndr)
                    for sp in range(nsp):
                        v.tensor_tensor(
                            mk.tmp1[:, 0:512],
                            mk.xstrip[:, slot * 1024 + sp * 512:
                                      slot * 1024 + sp * 512 + 512],
                            mk.ps_a[:, sp * 512:sp * 512 + 512],
                            ALU.subtract)
                        v.tensor_tensor(mk.tmp1[:, 0:512], mk.tmp1[:, 0:512],
                                        mk.ps_b[:, sp * 512:sp * 512 + 512],
                                        ALU.mult)
                        mk.inc(v.tensor_tensor(
                            mk.ostrip[:, sp * 512:sp * 512 + 512],
                            mk.tmp1[:, 0:512],
                            mk.ostrip[:, sp * 512:sp * 512 + 512],
                            ALU.add), "vp")
                self.on("vector", v_ap)
                nva = self.val("vp")

                def sy_o(sy, k=k, slot=slot, nva=nva, base=base):
                    mk.w(sy, "vp", nva)
                    dst = bass.AP(self.out_d, base + (128 * k) * S,
                                  [[S, 128], [1, TT]])
                    mk.inc(sy.dma_start(dst, mk.ostrip[:, 0:TT]), "d_out", 16)
                self.on("sync", sy_o)
                x2_free[slot] = ("d_out", self.val("d_out"))
                self.xstrip_free[slot] = ("vp", nva)
                self.ostrip_free[0] = ("d_out", self.val("d_out"))
            self.ps_ab_free = ("vp", self.val("vp"))

    def _finish(self):
        mk = self
        nout = self.val("d_out")

        def sy_end(sy):
            mk.w(sy, "d_out", nout)
        self.on("sync", sy_end)


def rl_w256(w):
    W = w.shape[1]
    return np.ascontiguousarray(
        w.reshape(16, 128, W // 256, 256).transpose(2, 1, 0, 3)
    ).reshape(-1, 16 * 256)


def rl_sw12(sw1t, sw2t):
    a = sw1t.reshape(16, 128, 64, 128).transpose(2, 1, 0, 3)
    b = sw2t.reshape(16, 128, 64, 128).transpose(2, 1, 0, 3)
    return np.ascontiguousarray(np.stack([a, b], axis=3)).reshape(
        64 * 128, 16 * 2 * 128)


def rl_w12(w12t):
    return np.ascontiguousarray(
        w12t.reshape(E, 16, 128, 8, 256).transpose(0, 3, 2, 1, 4)
    ).reshape(-1, 4096)


def rl_w3(w3t):
    return np.ascontiguousarray(
        w3t.reshape(E, 8, 128, 8, 256).transpose(0, 3, 2, 1, 4)
    ).reshape(-1, 2048)


def rl_sw3(sw3t):
    return np.ascontiguousarray(
        sw3t.reshape(2, 32, 128, 4, 512).transpose(3, 0, 2, 1, 4)
    ).reshape(-1, 32 * 512)


def kernel(L0_attn_x, L0_attn_Wq, L0_attn_Wf, L0_attn_Wi, L0_attn_Wo,
           L0_attn_g_weight, L0_ffn_w12, L0_ffn_w3, L0_ffn_router_w,
           L0_ffn_expert_bias, L0_ffn_sw1, L0_ffn_sw2, L0_ffn_sw3):
    S_CORE, TH = 4096, 512
    x = np.asarray(L0_attn_x, dtype=np.float32)
    B, SEQ, _ = x.shape
    NQ = SEQ // S_CORE

    mk = MK(S_CORE=S_CORE, TH=TH, TT=1024)
    nc = mk.build()

    shared = {
        "wq": rl_w256(np.asarray(L0_attn_Wq, np.float32)),
        "wf": rl_w256(np.asarray(L0_attn_Wf, np.float32)),
        "wi": rl_w256(np.asarray(L0_attn_Wi, np.float32)),
        "wo": rl_w256(np.asarray(L0_attn_Wo, np.float32)),
        "gwrow": np.ascontiguousarray(
            np.asarray(L0_attn_g_weight, np.float32)[None, :]),
        "sw12": rl_sw12(np.asarray(L0_ffn_sw1, np.float32).T,
                        np.asarray(L0_ffn_sw2, np.float32).T),
        "sw3t": rl_sw3(np.asarray(L0_ffn_sw3, np.float32).T),
        "w12t": rl_w12(np.ascontiguousarray(
            np.asarray(L0_ffn_w12, np.float32).transpose(0, 2, 1))),
        "w3t": rl_w3(np.ascontiguousarray(
            np.asarray(L0_ffn_w3, np.float32).transpose(0, 2, 1))),
        "rwt": np.ascontiguousarray(
            np.asarray(L0_ffn_router_w, np.float32).T),
        "ebias": np.ascontiguousarray(np.broadcast_to(
            np.asarray(L0_ffn_expert_bias, np.float32)[None, :], (128, E))),
    }

    in_maps = []
    for c in range(8):
        b, q = c // NQ, c % NQ
        main = x[b, q * S_CORE:(q + 1) * S_CORE, :]
        if q > 0:
            halo = x[b, q * S_CORE - TH:q * S_CORE, :]
            hon = np.ones((128, 1), np.float32)
        else:
            halo = np.zeros((TH, D), np.float32)
            hon = np.zeros((128, 1), np.float32)
        xt = np.ascontiguousarray(
            np.concatenate([halo, main], axis=0).T)
        m = dict(shared)
        m["x"] = xt
        m["halo_on"] = hon
        in_maps.append(m)

    res = run_bass_kernel_spmd(nc, in_maps, core_ids=list(range(8)))

    out = np.empty((B, SEQ, D), np.float32)
    for c in range(8):
        b, q = c // NQ, c % NQ
        out[b, q * S_CORE:(q + 1) * S_CORE, :] = res.results[c]["out"].T
    return out
